# version 13
# baseline (speedup 1.0000x reference)
import numpy as np
import ml_dtypes
import concourse.bass as bass
import concourse.mybir as mybir
from concourse.bass_utils import run_bass_kernel_spmd

F32 = mybir.dt.float32
BF16 = mybir.dt.bfloat16
I32 = mybir.dt.int32
AX = mybir.AxisListType.X
OP = mybir.AluOpType
AF = mybir.ActivationFunctionType
ET = mybir.EngineType
DS = bass.ds

B, S, D, H, DH = 2, 8192, 1024, 128, 4096
CH = 1024
KV = 9216
NBLK = KV // CH
GRID = 1152
OWNP = 2 * GRID
SCALE = float(np.float32(128.0 ** -0.5))
NEG = -1.0e30

ENGINES = ("tensor", "vector", "scalar", "gpsimd", "sync")


class Prog:

    def __init__(self, nc):
        self.nc = nc
        self.ops = []
        self.last_writer = {}
        self.readers = {}

    def op(self, engine, fn, reads=(), writes=(), dma=False, fence=()):
        i = len(self.ops)
        deps = set()
        for b in reads:
            if b in self.last_writer:
                deps.add(self.last_writer[b])
        for b in list(writes) + list(fence):
            if b in self.last_writer:
                deps.add(self.last_writer[b])
            for r in self.readers.get(b, ()):
                deps.add(r)
        deps.discard(i)
        self.ops.append((engine, fn, deps, dma))
        for b in writes:
            self.last_writer[b] = i
            self.readers[b] = []
        for b in reads:
            self.readers.setdefault(b, []).append(i)
        return i

    def emit(self, block):
        nc = self.nc
        n = len(self.ops)
        needed = [False] * n
        for (_, _, deps, _) in self.ops:
            for d in deps:
                needed[d] = True
        NROT = 8
        sems = {}
        for e in ENGINES:
            sems[(e, False)] = nc.alloc_semaphore(f"s_{e}_c")
            for r in range(NROT):
                sems[(e, True, r)] = nc.alloc_semaphore(f"s_{e}_d{r}")
        count = [0] * n
        slot = [None] * n
        cur_c = {e: 0 for e in ENGINES}
        cur_d = {(e, r): 0 for e in ENGINES for r in range(NROT)}
        nseen_d = {e: 0 for e in ENGINES}
        for i, (eng, fn, deps, isd) in enumerate(self.ops):
            if isd:
                r = nseen_d[eng] % NROT
                nseen_d[eng] += 1
                slot[i] = r
                cur_d[(eng, r)] += 16
                count[i] = cur_d[(eng, r)]
            elif needed[i]:
                cur_c[eng] += 1
                count[i] = cur_c[eng]
            else:
                count[i] = -1
        final_c = dict(cur_c)
        final_d = dict(cur_d)
        streams = {e: [] for e in ENGINES}
        waited = {e: {} for e in ENGINES}

        def add_wait(waits, eng, key, c):
            if waited[eng].get(key, 0) >= c:
                return
            waits.append((key, c))
            waited[eng][key] = c

        for i, (eng, fn, deps, isd) in enumerate(self.ops):
            waits = []
            if isd:
                key = (eng, True, slot[i])
                prior = count[i] - 16
                if prior > 0:
                    add_wait(waits, eng, key, prior)
            for d in sorted(deps):
                peng, pisd = self.ops[d][0], self.ops[d][3]
                c = count[d]
                if c < 0:
                    c = None
                    for k in range(d + 1, n):
                        if (self.ops[k][0], self.ops[k][3]) == (peng, pisd) and not pisd                                 and count[k] > 0:
                            c = count[k]
                            break
                    if c is None:
                        continue
                key = (peng, True, slot[d]) if pisd else (peng, False)
                add_wait(waits, eng, key, c)
            streams[eng].append((waits, fn, count[i] if count[i] > 0 else 0, isd,
                                 slot[i]))

        def make_body(e_name):
            def body(engine_obj):
                for waits, fn, cnt, isd, sl in streams[e_name]:
                    for key, c in waits:
                        engine_obj.wait_ge(sems[key], c)
                    inst = fn(engine_obj)
                    if inst is not None:
                        if isd:
                            inst.then_inc(sems[(e_name, True, sl)], 16)
                        elif cnt > 0:
                            inst.then_inc(sems[(e_name, False)], 1)
                for r in range(NROT):
                    if final_d[(e_name, r)] > 0:
                        engine_obj.wait_ge(sems[(e_name, True, r)], final_d[(e_name, r)])
            return body

        block.tensor(make_body("tensor"))
        block.vector(make_body("vector"))
        block.scalar(make_body("scalar"))
        block.gpsimd(make_body("gpsimd"))
        block.sync(make_body("sync"))


def build_program():
    nc = bass.Bass("TRN2", target_bir_lowering=False)
    p = Prog(nc)
    import bass_rust as _br

    def _vload(ap):
        v = nc.values_load(ap, engines=[ET.Pool], skip_runtime_bounds_check=True)
        return v

    def _vfree(v):
        nc.gpsimd.free_register(list(v.val)[0])

    x_kv = nc.dram_tensor("x_kv", [KV, D], F32, kind="ExternalInput").ap()
    x_own = nc.dram_tensor("x_own", [OWNP, D], F32, kind="ExternalInput").ap()
    wq_d = nc.dram_tensor("wq", [H, D], F32, kind="ExternalInput").ap()
    wk_d = nc.dram_tensor("wk", [H, D], F32, kind="ExternalInput").ap()
    wv_d = nc.dram_tensor("wv", [H, D], F32, kind="ExternalInput").ap()
    wo_d = nc.dram_tensor("wo", [D, H], F32, kind="ExternalInput").ap()
    bq_d = nc.dram_tensor("bq", [1, H], F32, kind="ExternalInput").ap()
    bk_d = nc.dram_tensor("bk", [1, H], F32, kind="ExternalInput").ap()
    bv_d = nc.dram_tensor("bv", [1, H], F32, kind="ExternalInput").ap()
    bo_d = nc.dram_tensor("bo", [1, D], F32, kind="ExternalInput").ap()
    w1_d = nc.dram_tensor("w1", [DH, D], F32, kind="ExternalInput").ap()
    w2_d = nc.dram_tensor("w2", [D, DH], F32, kind="ExternalInput").ap()
    cw_d = nc.dram_tensor("cw", [128, 96], F32, kind="ExternalInput").ap()
    cb_d = nc.dram_tensor("cb", [128, 32], F32, kind="ExternalInput").ap()
    sel_d = nc.dram_tensor("seltab", [1, 64], I32, kind="ExternalInput").ap()
    selw_d = nc.dram_tensor("selw_in", [128, 36], F32, kind="ExternalInput").ap()
    out_d = nc.dram_tensor("out_own", [2 * CH, D], F32, kind="ExternalOutput").ap()

    xq1_h = nc.dram_tensor("xq1_h", [OWNP, D], BF16).ap()
    w1q_h = nc.dram_tensor("w1q_h", [DH, D], BF16).ap()
    w2q_h = nc.dram_tensor("w2q_h", [D, DH], BF16).ap()
    kfm_h = nc.dram_tensor("kfm_h", [2, H, KV], BF16).ap()
    vtm_h = nc.dram_tensor("vtm_h", [2, 72 * H, 132], BF16).ap()
    qall_h = {hl: nc.dram_tensor(f"qall{hl}", [H, 2 * CH], BF16).ap() for hl in (0, 1)}
    part_h = nc.dram_tensor("part_h", [2 * CH, D], F32).ap()

    sb = lambda name, shape, dt=F32: nc.alloc_sbuf_tensor(name, shape, dt)
    ones_f = sb("ones_f", [128, 128])
    ones_b = sb("ones_b", [128, 128], BF16)
    identf = sb("identf", [128, 128])
    identb = sb("identb", [128, 128], BF16)
    tri_T = sb("tri_T", [128, 16 * 128], BF16)
    tri_qk = sb("tri_qk", [128, 16 * 128], BF16)
    wot = {0: sb("wot0", [128, D], BF16), 1: sb("wot1", [128, D], BF16)}
    brow = {nm: sb(f"brow_{nm}", [1, 128]) for nm in ("bq", "bk", "bv")}
    bo_row = sb("bo_row", [1, D])
    cw_sb = sb("cw_sb", [128, 3 * 32])
    cb_sb = sb("cb_sb", [128, 32])
    wsc = sb("wsc", [128, 4])
    sel_sb = nc.alloc_sbuf_tensor("sel_sb", [1, 64], I32)
    s1c = sb("s1c", [128, 18])
    s2c = sb("s2c", [128, 16])
    mAB = sb("mAB", [128, 32])
    selw = sb("selw", [128, 36])
    BFA = sb("BFA", [128, 32768], BF16)
    BFB = sb("BFB", [128, 16384], BF16)
    wqt = {}
    for ii, (nm, hl) in enumerate([(n, h) for n in "qkv" for h in (0, 1)]):
        wqt[(nm, hl)] = BFB[:, ii * 1024:(ii + 1) * 1024]
    FA = sb("FA", [128, 8192], F32)
    trash = tri_T[:, 0:1024]
    xnf = sb("xnf", [128, 1024])
    sm1 = sb("sm1", [128, 1024])
    sm2 = sb("sm2", [128, 1024])
    sm4 = sb("sm4", [128, 1024])
    smi = nc.alloc_sbuf_tensor("smi", [128, 1024], I32)
    evA = sb("evA", [128, 1024])
    bh1 = sb("bh1", [128, 1024], BF16)
    bh2 = sb("bh2", [128, 1024], BF16)
    khi_b = BFB[:, 6144:7168]
    klo_b = BFB[:, 7168:8192]
    bh1b = BFB[:, 8192:9216]
    bh2b = BFB[:, 9216:10240]
    xqf = sb("xqf", [128, 8 * 272], BF16)
    hts = [sb("hts0", [128, 258]), sb("hts1", [128, 258])]
    cv1 = [sb("cv1_0", [128, 256]), sb("cv1_1", [128, 256])]
    cv2 = [sb("cv2_0", [128, 256]), sb("cv2_1", [128, 256])]
    s1bc = sb("s1bc", [128, 384])
    ascbc = s1bc
    M1 = [sb("M1_0", [128, 256]), sb("M1_1", [128, 256])]
    M1n = [sb("M1n_0", [128, 256]), sb("M1n_1", [128, 256])]
    diagt = sb("diagt", [128, 128])
    diagS = xqf[:, 0:1024]
    vtt = {0: sb("vtt0", [128, 132], BF16), 1: sb("vtt1", [128, 132], BF16)}
    nmst = sb("nmst", [128, 8])
    mloc = sb("mloc", [128, 8])
    colsc = sb("colsc", [128, 24])
    ch05 = sb("ch05", [128, 1024])
    dsc = sb("dsc", [128, 64])
    PS = [nc.alloc_psum_tensor(f"PS{i}", [128, 512], F32) for i in range(8)]
    psT = PS[7]

    def mm512(pair, lhsT, rhs1024, start, stop, reads, wkeys):
        for h in range(2):
            p.op("tensor", lambda e, h=h, lhsT=lhsT, rhs1024=rhs1024, start=start, stop=stop: e.matmul(
                pair[h][:], lhsT, rhs1024[:, h * 512:(h + 1) * 512], start=start, stop=stop),
                reads=list(reads), writes=[wkeys[h]])

    p.op("gpsimd", lambda e: e.memset(ones_f[:], 1.0), writes=["ones_f"])
    p.op("gpsimd", lambda e: e.memset(ones512[:], 1.0), writes=["ones512"])
    p.op("gpsimd", lambda e: e.memset(ones_b[:], 1.0), writes=["ones_b"])
    p.op("gpsimd", lambda e: e.affine_select(identf[:], ones_f[:], [[-1, 128]], OP.is_equal,
                                             0.0, channel_multiplier=1),
         reads=["ones_f"], writes=["identf"])
    p.op("vector", lambda e: e.tensor_copy(identb[:], identf[:]), reads=["identf"], writes=["identb"])
    def build_tri_masks():
        p.op("gpsimd", lambda e: e.memset(tri_T[:], 0.0), writes=["tri_T"], fence=["trash"])
        p.op("gpsimd", lambda e: e.memset(tri_qk[:], 0.0), writes=["tri_qk"])
        p.op("gpsimd", lambda e: e.affine_select(tri_T.rearrange("p (c q) -> p c q", c=16),
                                                 tri_T.rearrange("p (c q) -> p c q", c=16),
                                                 [[128, 16], [1, 128]], OP.is_ge, NEG,
                                                 base=-8 * 128, channel_multiplier=-1),
             reads=["tri_T"], writes=["tri_T"])
        p.op("gpsimd", lambda e: e.affine_select(tri_qk.rearrange("p (c k) -> p c k", c=16),
                                                 tri_qk.rearrange("p (c k) -> p c k", c=16),
                                                 [[-128, 16], [-1, 128]], OP.is_ge, NEG,
                                                 base=8 * 128, channel_multiplier=1),
             reads=["tri_qk"], writes=["tri_qk"])
    p.op("sync", lambda e: e.dma_start(out=sel_sb[:], in_=sel_d), writes=["sel_sb"], dma=True)
    p.op("sync", lambda e: e.dma_start(out=selw[:], in_=selw_d), writes=["selw"], dma=True)
    p.op("sync", lambda e: e.dma_start(out=bo_row[:], in_=bo_d), writes=["bo_row"], dma=True)
    for nm, t in (("bq", bq_d), ("bk", bk_d), ("bv", bv_d)):
        p.op("sync", lambda e, t=t, d=brow[nm]: e.dma_start(out=d[:], in_=t),
             writes=[nm], dma=True)
    p.op("sync", lambda e: e.dma_start(out=cw_sb[:], in_=cw_d), writes=["cw_sb"], dma=True)
    p.op("sync", lambda e: e.dma_start(out=cb_sb[:], in_=cb_d), writes=["cb_sb"], dma=True)

    wtmp, whib, wlob = ch05, bh1, bh2

    def small_w_transpose(srcfn, srckey, dsts, dstkeys):
        for c in range(8):
            p.op("tensor", lambda e, c=c: e.transpose(
                psT[:, 0:128], wtmp[:, c * 128:(c + 1) * 128], identf[:]),
                reads=[srckey, "identf"], writes=["PSb7a"])
            p.op("scalar", lambda e, d=dsts[0], c=c: e.activation(
                d[:, c * 128:(c + 1) * 128], psT[:, 0:128], AF.Copy),
                reads=["PSb7a"], writes=[dstkeys[0]])
            p.op("vector", lambda e, d0=dsts[0], d1=dsts[1], c=c: e.scalar_tensor_tensor(
                d1[:, c * 128:(c + 1) * 128], d0[:, c * 128:(c + 1) * 128], -1.0,
                psT[:, 0:128], OP.mult, OP.add),
                reads=["PSb7a", dstkeys[0]], writes=[dstkeys[1]])

    for nm, t in (("q", wq_d), ("k", wk_d), ("v", wv_d)):
        p.op("sync", lambda e, t=t: e.dma_start(out=wtmp[:], in_=t), writes=["ch05"], dma=True)
        small_w_transpose(None, "ch05", {0: wqt[(nm, 0)], 1: wqt[(nm, 1)]},
                          {0: f"wqt_{nm}0", 1: f"wqt_{nm}1"})
    p.op("sync", lambda e: e.dma_start(out=wtmp.rearrange("p (c h) -> p c h", c=8),
                                       in_=wo_d.rearrange("(c p) h -> p c h", p=128)),
         writes=["ch05"], dma=True)
    small_w_transpose(None, "ch05", wot, {0: "wot0", 1: "wot1"})
    p.op("gpsimd", lambda e: e.memset(ch05[:], 0.5), writes=["ch05"])

    qbuf = [sb("qbufA", [128, 512]), sb("qbufB", [128, 512])]
    ones512 = sb("ones512", [1, 512])
    qge = sb("qge", [128, 512], BF16)
    qle = sb("qle", [128, 512], BF16)
    qsub = [sb("qsubA", [128, 512], BF16), sb("qsubB", [128, 512], BF16)]

    def mk_quant_units(wsrc, n_rows, n_cols, dst_h, sc_col, tag):
        ntile = n_rows // 128
        nch = n_cols // 512
        units = []
        units.append(lambda: p.op("vector", lambda e: e.memset(colsc[:, 0:1], 0.0),
                                  writes=["colsc0"], fence=["colsc1"]))
        for i, (t0, ch) in enumerate([(t, c) for t in range(ntile) for c in range(nch)]):
            def em(t0=t0, ch=ch, par=i % 2, q="sync"):
                b = qbuf[par]
                p.op(q, lambda e, t0=t0, ch=ch, b=b: e.dma_start(
                    out=b[:], in_=wsrc[t0 * 128:(t0 + 1) * 128, ch * 512:(ch + 1) * 512]),
                    writes=[f"qbuf{par}"], dma=True)
                p.op("vector", lambda e, b=b: e.tensor_reduce(
                    colsc[:, 1:2], b[:], AX, OP.add, apply_absolute_value=True),
                    reads=[f"qbuf{par}"], writes=["colsc1"])
                p.op("vector", lambda e: e.tensor_tensor(
                    colsc[:, 0:1], colsc[:, 0:1], colsc[:, 1:2], OP.add),
                    reads=["colsc1", "colsc0"], writes=["colsc0"])
            units.append(em)

        def fin():
            p.op("tensor", lambda e: e.matmul(PS[6][0:1, 0:1], ones_f[:, 0:1], colsc[:, 0:1],
                                              start=True, stop=True),
                 reads=["colsc0", "ones_f"], writes=["PSb6"])
            p.op("vector", lambda e, nn=float(n_rows * n_cols): e.tensor_scalar(
                colsc[0:1, 2:3], PS[6][0:1, 0:1], 1.0 / nn, 1e-5, OP.mult, OP.max),
                reads=["PSb6"], writes=["colsc2"])
            p.op("vector", lambda e: e.reciprocal(colsc[0:1, 3:4], colsc[0:1, 2:3]),
                 reads=["colsc2"], writes=["colsc3"])
            p.op("tensor", lambda e: e.matmul(PS[6][:, 1:2], ones_f[0:1, :], colsc[0:1, 3:4],
                                              start=True, stop=True),
                 reads=["colsc3", "ones_f"], writes=["PSb6"])
            p.op("tensor", lambda e: e.matmul(PS[6][:, 2:3], ones_f[0:1, :], colsc[0:1, 2:3],
                                              start=True, stop=True),
                 reads=["colsc2", "ones_f"], writes=["PSb6"])
            p.op("vector", lambda e, sc=sc_col: e.tensor_copy(wsc[:, sc:sc + 1], PS[6][:, 1:2]),
                 reads=["PSb6"], writes=[f"wsc{sc_col}"])
            p.op("vector", lambda e, sc=sc_col: e.tensor_copy(wsc[:, sc + 2:sc + 3], PS[6][:, 2:3]),
                 reads=["PSb6"], writes=[f"wsc{sc_col + 2}"])
        units.append(fin)
        for i, (t0, ch) in enumerate([(t, c) for t in range(ntile) for c in range(nch)]):
            def em2(t0=t0, ch=ch, par=i % 2, q="sync"):
                b = qbuf[par]
                p.op(q, lambda e, t0=t0, ch=ch, b=b: e.dma_start(
                    out=b[:], in_=wsrc[t0 * 128:(t0 + 1) * 128, ch * 512:(ch + 1) * 512]),
                    writes=[f"qbuf{par}"], dma=True)
                p.op("vector", lambda e, b=b, sc=sc_col: e.tensor_scalar(
                    qge[:], b[:], wsc[:, sc:sc + 1], 0.5, OP.mult, OP.is_ge),
                    reads=[f"qbuf{par}", f"wsc{sc_col}"], writes=["qge"])
                p.op("vector", lambda e, b=b, sc=sc_col: e.tensor_scalar(
                    qle[:], b[:], wsc[:, sc:sc + 1], -0.5, OP.mult, OP.is_le),
                    reads=[f"qbuf{par}", f"wsc{sc_col}"], writes=["qle"])
                p.op("vector", lambda e, par=par: e.tensor_tensor(
                    qsub[par][:], qle[:], qge[:], OP.subtract),
                    reads=["qge", "qle"], writes=[f"qsub{par}"])
                p.op("sync", lambda e, t0=t0, ch=ch, par=par: e.dma_start(
                    out=dst_h[t0 * 128:(t0 + 1) * 128, ch * 512:(ch + 1) * 512],
                    in_=qsub[par][:]),
                    reads=[f"qsub{par}"], writes=[f"wqH_{tag}"], dma=True)
            units.append(em2)
        return units

    xb8 = [FA[:, i * 1024:(i + 1) * 1024] for i in range(8)]

    def ln_loads(xsrc, t0, half, nt):
        for i in range(nt):
            ti = half * 4 + i
            p.op("gpsimd", lambda e, row=(t0 + i) * 128, ti=ti, xsrc=xsrc: e.dma_start(
                out=xb8[ti], in_=xsrc[row:row + 128, :]),
                writes=[f"lnx_{ti}"], dma=True, fence=["FA", "FA2"])

    def ln_stats4(half, nt):
        K = f"dsc{half}"
        for i in range(nt):
            ti = half * 4 + i
            p.op("scalar", lambda e, ti=ti: e.activation(trash[:], xb8[ti][:], AF.Copy,
                                                         accum_out=dsc[:, ti:ti + 1]),
                 reads=[f"lnx_{ti}"], writes=[K, "trash"])
            p.op("scalar", lambda e, ti=ti: e.activation(trash[:], xb8[ti][:], AF.Square,
                                                         accum_out=dsc[:, 8 + ti:9 + ti]),
                 reads=[f"lnx_{ti}"], writes=[K, "trash"])

    def ln_chain4(half, nt):
        c = dsc
        K = f"dsc{half}"
        b = half * 4
        sm, sv, st = 32 + half * 16, 36 + half * 16, 40 + half * 16

        def v(fn):
            p.op("vector", fn, reads=[K], writes=[K])
        v(lambda e: e.tensor_scalar(c[:, sm:sm + nt], c[:, b:b + nt], 1.0 / D, None, OP.mult))
        v(lambda e: e.tensor_tensor(c[:, sv:sv + nt], c[:, sm:sm + nt], c[:, sm:sm + nt], OP.mult))
        v(lambda e: e.scalar_tensor_tensor(c[:, sv:sv + nt], c[:, 8 + b:8 + b + nt], 1.0 / D,
                                           c[:, sv:sv + nt], OP.mult, OP.subtract))
        v(lambda e: e.tensor_scalar(c[:, sv:sv + nt], c[:, sv:sv + nt], 1e-5, None, OP.add))
        p.op("scalar", lambda e: e.activation(c[:, st:st + nt], c[:, sv:sv + nt], AF.Sqrt),
             reads=[K], writes=[K])
        v(lambda e: e.reciprocal(c[:, 16 + b:16 + b + nt], c[:, st:st + nt]))
        v(lambda e: e.tensor_tensor(c[:, st:st + nt], c[:, 16 + b:16 + b + nt],
                                    c[:, 16 + b:16 + b + nt], OP.mult))
        v(lambda e: e.tensor_tensor(c[:, st:st + nt], c[:, st:st + nt], c[:, sv:sv + nt], OP.mult))
        v(lambda e: e.tensor_scalar(c[:, st:st + nt], c[:, st:st + nt], -0.5, 1.5, OP.mult, OP.add))
        v(lambda e: e.tensor_tensor(c[:, 16 + b:16 + b + nt], c[:, 16 + b:16 + b + nt],
                                    c[:, st:st + nt], OP.mult))
        v(lambda e: e.tensor_tensor(c[:, 24 + b:24 + b + nt], c[:, sm:sm + nt],
                                    c[:, 16 + b:16 + b + nt], OP.mult))
        v(lambda e: e.tensor_scalar(c[:, 24 + b:24 + b + nt], c[:, 24 + b:24 + b + nt],
                                    -1.0, None, OP.mult))

    lnpair = {0: (bh1, bh2, "bh1", "bh2"), 1: (bh1b, bh2b, "bh1B", "bh2B")}
    xnfp = {0: (xnf, "xnf"), 1: (sm1, "sm1")}

    def ln_apply4(ti, half):
        K = f"dsc{half}"
        xb, xk = xnfp[ti % 2]
        p.op("vector", lambda e, ti=ti, xb=xb: e.tensor_scalar(xb[:], xb8[ti][:], dsc[:, 16 + ti:17 + ti],
                                                               dsc[:, 24 + ti:25 + ti], OP.mult, OP.add),
             reads=[f"lnx_{ti}", K], writes=[xk])

    def kv_post(t):
        xb, xk = xnfp[t % 2]
        par = (t // 8) % 2
        j = t % 8
        for c in range(8):
            ps = psT[:, 0:128] if c % 2 == 0 else PS[6][:, 0:128]
            psk = "PSb7a" if c % 2 == 0 else "PSb6"
            p.op("tensor", lambda e, c=c, xb=xb, ps=ps: e.transpose(
                ps, xb[:, c * 128:(c + 1) * 128], identf[:]),
                reads=[xk, "identf"], writes=[psk])
            p.op("scalar", lambda e, c=c, ps=ps, par=par, j=j: e.activation(
                xnfm2[par][(c, 0)][:, j * 128:(j + 1) * 128], ps, AF.Copy),
                reads=[psk], writes=[f"xnfm_{par}_{c}_0"])
            p.op("vector", lambda e, c=c, ps=ps, par=par, j=j: e.scalar_tensor_tensor(
                xnfm2[par][(c, 1)][:, j * 128:(j + 1) * 128],
                xnfm2[par][(c, 0)][:, j * 128:(j + 1) * 128], -1.0, ps, OP.mult, OP.add),
                reads=[psk, f"xnfm_{par}_{c}_0"], writes=[f"xnfm_{par}_{c}_1"])

    def own_post(t):
        xb, xk = xnfp[t % 2]
        xnf_l = xb
        p.op("vector", lambda e, xnf=xnf_l: e.tensor_reduce(colsc[:, 14:15], xnf[:], AX, OP.max,
                                                            apply_absolute_value=True),
             reads=[xk], writes=["colscq"])
        p.op("vector", lambda e: e.tensor_scalar(colsc[:, 15:16], colsc[:, 14:15], 1e-5, None, OP.max),
             reads=["colscq"], writes=["colscq"])
        p.op("vector", lambda e: e.reciprocal(colsc[:, 16:17], colsc[:, 15:16]),
             reads=["colscq"], writes=["colscq"])
        p.op("vector", lambda e: e.tensor_scalar(colsc[:, 17:18], colsc[:, 16:17], 127.0, None, OP.mult),
             reads=["colscq"], writes=["colscq"])
        p.op("vector", lambda e, t=t: e.scalar_tensor_tensor(
            s1c[:, t:t + 1], colsc[:, 15:16], 1.0 / 127.0, wsc[:, 2:3], OP.mult, OP.mult),
            reads=["colscq", "wsc2"], writes=["s1c"])
        p.op("vector", lambda e, xnf=xnf_l: e.tensor_scalar(sm4[:], xnf[:], colsc[:, 17:18], None, OP.mult),
             reads=[xk, "colscq"], writes=["sm4"])
        p.op("vector", lambda e, xnf=xnf_l: e.tensor_scalar(sm2[:], xnf[:], 0.0, 0.5, OP.is_ge, OP.subtract),
             reads=[xk], writes=["sm2"])
        p.op("vector", lambda e: e.tensor_tensor(smi[:], sm4[:], sm2[:], OP.add),
             reads=["sm4", "sm2"], writes=["smi"])
        p.op("vector", lambda e: e.tensor_tensor(sm4[:], sm4[:], smi[:], OP.subtract),
             reads=["smi", "sm4"], writes=["sm4"])
        p.op("vector", lambda e: e.tensor_tensor(sm2[:], sm4[:], ch05[:], OP.is_gt),
             reads=["sm4", "ch05"], writes=["sm2"])
        p.op("vector", lambda e: e.tensor_tensor(sm2[:], sm2[:], smi[:], OP.add),
             reads=["sm2", "smi"], writes=["sm2"])
        hb, lb, hk, lk = lnpair[t % 2]
        p.op("vector", lambda e, hb=hb: e.scalar_tensor_tensor(hb[:], sm4[:], -0.5, sm2[:], OP.is_lt, OP.subtract),
             reads=["sm4", "sm2"], writes=[hk])
        p.op("gpsimd", lambda e, t=t, hb=hb: e.dma_start(out=xq1_h[t * 128:(t + 1) * 128, :], in_=hb[:]),
             reads=[hk], writes=[f"xq1H_{t}"], dma=True)

    def ln_body(xsrc, t0, half, nt, post):
        ln_stats4(half, nt)
        ln_chain4(half, nt)
        for i in range(nt):
            ln_apply4(half * 4 + i, half)
            post(t0 + i)

    _lnq = {"pending": None, "half": 0}

    def sched_group(xsrc, t0, nt, post):
        h = _lnq["half"]
        _lnq["half"] ^= 1
        ln_loads(xsrc, t0, h, nt)
        prev = _lnq["pending"]
        _lnq["pending"] = (xsrc, t0, h, nt, post)
        if prev is not None:
            ln_body(*prev)

    def sched_flush():
        prev = _lnq["pending"]
        _lnq["pending"] = None
        if prev is not None:
            ln_body(*prev)

    xnfm2 = {par: {(c, hl): BFA[:, par * 16384 + (hl * 8 + c) * 1024:
                                 par * 16384 + (hl * 8 + c + 1) * 1024]
                   for c in range(8) for hl in (0, 1)} for par in (0, 1)}
    p.op("gpsimd", lambda e: e.memset(vtt[0][:, 128:132], 0.0), writes=["vtt0c"])
    p.op("gpsimd", lambda e: e.memset(vtt[0][:, 128:129], 1.0), writes=["vtt0c"], fence=["vtt0c"])
    p.op("gpsimd", lambda e: e.memset(vtt[1][:, 128:132], 0.0), writes=["vtt1c"])
    PQ = (PS[0], PS[1])
    PK = (PS[2], PS[3])
    PV_ = (PS[4], PS[5])
    pkeys = {id(PS[i]): f"PSb{i}" for i in range(8)}

    def qkv_block(b):
        par = b % 2
        xnfm = xnfm2[par]
        for nm, pair, brk in (("q", PQ, "bq"), ("k", PK, "bk"), ("v", PV_, "bv")):
            for hh in range(2):
                p.op("tensor", lambda e, pair=pair, nm=nm, hh=hh, brk=brk: e.matmul(
                    pair[hh][:], brow[brk][0:1, :], ones512[0:1, :],
                    start=True, stop=False),
                    reads=[brk, "ones512"], writes=[pkeys[id(pair[hh])]])
            passes = [(0, 0), (1, 0), (0, 1)]
            for pi, (whl, xhl) in enumerate(passes):
                for c in range(8):
                    sp = (pi == 2 and c == 7)
                    for hh in range(2):
                        p.op("tensor", lambda e, pair=pair, nm=nm, whl=whl, xhl=xhl, c=c, sp=sp, hh=hh, xnfm=xnfm: e.matmul(
                            pair[hh][:], wqt[(nm, whl)][:, c * 128:(c + 1) * 128],
                            xnfm[(c, xhl)][:, hh * 512:(hh + 1) * 512], start=False, stop=sp),
                            reads=[f"wqt_{nm}{whl}", f"xnfm_{par}_{c}_{xhl}"],
                            writes=[pkeys[id(pair[hh])]])
        tag = f"qkv{b}"
        for nm, pair in (("k", PK), ("v", PV_), ("q", PQ)):
            if nm != "v":
                for hh in range(2):
                    p.op("scalar", lambda e, pair=pair, hh=hh: e.activation(
                        khi_b[:, hh * 512:(hh + 1) * 512], pair[hh][:], AF.Copy),
                        reads=[pkeys[id(pair[hh])]], writes=["khiB"])
                    p.op("vector", lambda e, pair=pair, hh=hh: e.scalar_tensor_tensor(
                        klo_b[:, hh * 512:(hh + 1) * 512], khi_b[:, hh * 512:(hh + 1) * 512],
                        -1.0, pair[hh][:], OP.mult, OP.add),
                        reads=["khiB", pkeys[id(pair[hh])]], writes=["kloB"])
            else:
                for hh in range(2):
                    p.op("scalar", lambda e, pair=pair, hh=hh: e.activation(
                        evA[:, hh * 512:(hh + 1) * 512], pair[hh][:], AF.Copy),
                        reads=[pkeys[id(pair[hh])]], writes=["evA"])
            if nm == "k":
                p.op("sync", lambda e, b=b: e.dma_start(out=kfm_h[0, :, b * CH:(b + 1) * CH], in_=khi_b[:]),
                     reads=["khiB"], writes=[f"kfmH0_{b}"], dma=True)
                p.op("sync", lambda e, b=b: e.dma_start(out=kfm_h[1, :, b * CH:(b + 1) * CH], in_=klo_b[:]),
                     reads=["kloB"], writes=[f"kfmH1_{b}"], dma=True)
            elif nm == "v":
                for i in range(8):
                    p.op("tensor", lambda e, i=i: e.transpose(
                        psT[:, 0:128], evA[:, i * 128:(i + 1) * 128], identf[:]),
                        reads=["evA", "identf"], writes=["PSb7a"])
                    p.op("scalar", lambda e: e.activation(vtt[0][:, 0:128], psT[:, 0:128], AF.Copy),
                         reads=["PSb7a"], writes=["vtt0"], fence=["vtt0c"])
                    p.op("vector", lambda e: e.scalar_tensor_tensor(
                        vtt[1][:, 0:128], vtt[0][:, 0:128], -1.0, psT[:, 0:128], OP.mult, OP.add),
                        reads=["PSb7a", "vtt0"], writes=["vtt1"], fence=["vtt1c"])
                    for hl in (0, 1):
                        p.op("sync", lambda e, hl=hl, b=b, i=i: e.dma_start(
                            out=vtm_h[hl, (b * 8 + i) * 128:(b * 8 + i + 1) * 128, :], in_=vtt[hl][:]),
                            reads=[f"vtt{hl}", f"vtt{hl}c"], writes=[f"vtmH_{hl}_{b}_{i}"], dma=True)
            else:
                if b >= 7:
                    p.op("sync", lambda e, b=b: e.dma_start(
                        out=qall_h[0][:, (b - 7) * CH:(b - 6) * CH], in_=khi_b[:]),
                        reads=["khiB"], writes=[f"qallH0_{b}"], dma=True)
                    p.op("sync", lambda e, b=b: e.dma_start(
                        out=qall_h[1][:, (b - 7) * CH:(b - 6) * CH], in_=klo_b[:]),
                        reads=["kloB"], writes=[f"qallH1_{b}"], dma=True)

    uq = (mk_quant_units(w1_d, DH, D, w1q_h, 0, "w1")
          + mk_quant_units(w2_d, D, DH, w2q_h, 1, "w2"))
    kv_order = [7, 8, 0, 1, 2, 3, 4, 5, 6]
    own_groups = [(0, 4), (4, 4), (8, 4), (12, 4), (16, 2)]
    per = (len(uq) + 8) // 9
    for idx, b in enumerate(kv_order):
        sched_group(x_kv, b * 8, 4, kv_post)
        sched_group(x_kv, b * 8 + 4, 4, kv_post)
        if idx >= 4:
            og = own_groups[idx - 4]
            sched_group(x_own, og[0], og[1], own_post)
        if idx >= 1:
            qkv_block(kv_order[idx - 1])
        for u in uq[idx * per:(idx + 1) * per]:
            u()
    sched_flush()
    qkv_block(kv_order[-1])
    for u in uq[9 * per:]:
        u()

    qall_done = [f"qallH{hl}_{b}" for hl in (0, 1) for b in (7, 8)]
    kv_keys = [f"kfmH{hl}_{b}" for hl in (0, 1) for b in range(NBLK)]
    vt_keys = [f"vtmH_{hl}_{b}_{i}" for hl in (0, 1) for b in range(NBLK) for i in range(8)]

    w1t_keys = []
    for c in range(8):
        kk = f"W1T_{c}"
        w1t_keys.append(kk)
        p.op("sync", lambda e, c=c: e.dma_start(
            out=BFA[:, c * 4096:(c + 1) * 4096], in_=w1q_h[:, c * 128:(c + 1) * 128],
            transpose=True),
            reads=["wqH_w1"], writes=[kk], dma=True,
            fence=[f"xnfm_{par}_{cc}_{hl}" for par in (0, 1) for cc in range(8) for hl in (0, 1)])
    g_all = FA
    nq_sub = BFB[:, 0:8192]
    w2qt = BFB[:, 8192:16384]

    for cc in range(2):
        for sub in range(4):
            g0 = sub * 256
            stag = f"f{cc}_{sub}"
            r0 = cc * GRID + g0
            for c in range(8):
                p.op("sync", lambda e, c=c, r0=r0: e.dma_start(
                    out=xqf[:, c * 272:(c + 1) * 272],
                    in_=xq1_h[r0:r0 + 272, c * 128:(c + 1) * 128], transpose=True),
                    reads=[f"xq1H_{r0 // 128 + i}" for i in range(3)],
                    writes=["xqf"], dma=True)
            for i in range(3):
                tl = cc * 9 + 2 * sub + i
                p.op("vector", lambda e, tl=tl: e.tensor_tensor(
                    diagt[:], identf[:], s1c[:, tl:tl + 1].broadcast_to([128, 128]), OP.mult),
                    reads=["s1c", "identf"], writes=["diagt"])
                p.op("tensor", lambda e: e.matmul(PS[1][:, 0:128], ones_f[:], diagt[:],
                                                  start=True, stop=True),
                     reads=["diagt", "ones_f"], writes=["PSb1"])
                p.op("vector", lambda e, i=i: e.tensor_copy(s1bc[:, i * 128:(i + 1) * 128],
                                                            PS[1][:, 0:128]),
                     reads=["PSb1"], writes=["s1bc"])
            for par in range(2):
                p.op("gpsimd", lambda e, par=par: e.memset(M1[par][:], 0.0), writes=[f"M1_{par}"],
                     fence=["FA2"])
                p.op("gpsimd", lambda e, par=par: e.memset(M1n[par][:], 0.0), writes=[f"M1n_{par}"])
            def mm_minmax(hid):
                par = hid % 2
                p.op("vector", lambda e, hid=hid, par=par: e.tensor_tensor(
                    M1[par][:], M1[par][:], g_all[:, hid * 256:(hid + 1) * 256], OP.max),
                    reads=[f"gq_{hid // 4}", f"M1_{par}"], writes=[f"M1_{par}"])
                p.op("vector", lambda e, hid=hid, par=par: e.tensor_tensor(
                    M1n[par][:], M1n[par][:], g_all[:, hid * 256:(hid + 1) * 256], OP.min),
                    reads=[f"gq_{hid // 4}", f"M1n_{par}"], writes=[f"M1n_{par}"])

            for hid in range(32):
                par = hid % 2
                psh = PS[0] if par == 0 else PS[6]
                pshk = "PSb0" if par == 0 else "PSb6"
                for c in range(8):
                    p.op("tensor", lambda e, c=c, hid=hid, psh=psh: e.matmul(
                        psh[:, 0:258], BFA[:, c * 4096 + hid * 128:c * 4096 + (hid + 1) * 128],
                        xqf[:, c * 272:c * 272 + 258], start=(c == 0), stop=(c == 7)),
                        reads=["xqf", f"W1T_{c}"], writes=[pshk])
                p.op("vector", lambda e, par=par, psh=psh: e.tensor_tensor(
                    hts[par][:], psh[:, 0:258], s1bc[:, 0:258], OP.mult),
                    reads=[pshk, "s1bc"], writes=[f"hts{par}"])
                p.op("vector", lambda e, hid=hid, par=par: e.tensor_scalar(
                    cv1[par][:], hts[par][:, 1:257], cw_sb[:, 32 + hid:33 + hid], None, OP.mult),
                    reads=[f"hts{par}", "cw_sb"], writes=[f"cv1_{par}"])
                p.op("vector", lambda e, hid=hid, par=par: e.tensor_scalar(
                    cv2[par][:], hts[par][:, 0:256], cw_sb[:, hid:hid + 1], None, OP.mult),
                    reads=[f"hts{par}", "cw_sb"], writes=[f"cv2_{par}"])
                p.op("vector", lambda e, par=par: e.tensor_tensor(cv2[par][:], cv2[par][:], cv1[par][:], OP.add),
                    reads=[f"cv2_{par}", f"cv1_{par}"], writes=[f"cv2_{par}"])
                p.op("vector", lambda e, hid=hid, par=par: e.scalar_tensor_tensor(
                    cv1[par][:], hts[par][:, 2:258], cw_sb[:, 64 + hid:65 + hid], cv2[par][:],
                    OP.mult, OP.add),
                    reads=[f"hts{par}", f"cv2_{par}", "cw_sb"], writes=[f"cv1_{par}"])
                lnf = ([f"lnx_{hid // 4}"] if (cc, sub) == (0, 0) and hid % 4 == 0 else [])
                p.op("scalar", lambda e, hid=hid, par=par: e.activation(
                    g_all[:, hid * 256:(hid + 1) * 256], cv1[par][:], AF.Silu,
                    bias=cb_sb[:, hid:hid + 1]),
                    reads=[f"cv1_{par}", "cb_sb"], writes=[f"gq_{hid // 4}"], fence=lnf)
                if hid > 0:
                    mm_minmax(hid - 1)
            mm_minmax(31)
            p.op("vector", lambda e: e.tensor_tensor(M1[0][:], M1[0][:], M1[1][:], OP.max),
                 reads=["M1_0", "M1_1"], writes=["M1_0"])
            p.op("vector", lambda e: e.tensor_tensor(M1n[0][:], M1n[0][:], M1n[1][:], OP.min),
                 reads=["M1n_0", "M1n_1"], writes=["M1n_0"])
            for i in range(2):
                p.op("tensor", lambda e, i=i: e.transpose(psT[:, 0:128], M1[0][:, i * 128:(i + 1) * 128],
                                                          identf[:]),
                     reads=["M1_0", "identf"], writes=["PSb7a"])
                p.op("vector", lambda e: e.tensor_reduce(colsc[:, 4:5], psT[:, 0:128], AX, OP.max),
                     reads=["PSb7a"], writes=["colsc45"])
                p.op("tensor", lambda e, i=i: e.transpose(psT[:, 0:128], M1n[0][:, i * 128:(i + 1) * 128],
                                                          identf[:]),
                     reads=["M1n_0", "identf"], writes=["PSb7a"], fence=["colsc45"])
                p.op("vector", lambda e: e.tensor_reduce(colsc[:, 3:4], psT[:, 0:128], AX, OP.min,
                                                         negate=True),
                     reads=["PSb7a"], writes=["colsc34"])
                p.op("vector", lambda e: e.tensor_tensor(colsc[:, 4:5], colsc[:, 4:5], colsc[:, 3:4],
                                                         OP.max),
                     reads=["colsc34", "colsc45"], writes=["colsc45"])
                p.op("vector", lambda e: e.tensor_scalar(colsc[:, 5:6], colsc[:, 4:5], 1e-5, None, OP.max),
                     reads=["colsc45"], writes=["colsc56"])
                p.op("vector", lambda e: e.reciprocal(colsc[:, 6:7], colsc[:, 5:6]),
                     reads=["colsc56"], writes=["colsc67"])
                p.op("vector", lambda e, i=i: e.tensor_scalar(colsc[:, 7 + i:8 + i], colsc[:, 6:7],
                                                              127.0, None, OP.mult),
                     reads=["colsc67"], writes=["colsc78"])
                tl = cc * 8 + 2 * sub + i
                p.op("vector", lambda e, tl=tl: e.scalar_tensor_tensor(
                    s2c[:, tl:tl + 1], colsc[:, 5:6], 1.0 / 127.0, wsc[:, 3:4], OP.mult, OP.mult),
                    reads=["colsc56", "wsc3"], writes=["s2c"])
                p.op("vector", lambda e, i=i: e.tensor_tensor(
                    diagt[:], identf[:], colsc[:, 7 + i:8 + i].broadcast_to([128, 128]), OP.mult),
                    reads=["colsc78", "identf"], writes=["diagt"])
                p.op("tensor", lambda e: e.matmul(PS[1][:, 0:128], ones_f[:], diagt[:],
                                                  start=True, stop=True),
                     reads=["diagt", "ones_f"], writes=["PSb1"])
                p.op("vector", lambda e, i=i: e.tensor_copy(ascbc[:, i * 128:(i + 1) * 128],
                                                            PS[1][:, 0:128]),
                     reads=["PSb1"], writes=["s1bc"])
            for gidx in range(8):
                gk = f"gq_{gidx}"
                gsl = g_all[:, gidx * 1024:(gidx + 1) * 1024]
                a3 = ascbc[:, 0:256].rearrange("p t -> p () t").broadcast_to([128, 4, 256])
                p.op("vector", lambda e, gsl=gsl: e.tensor_scalar(
                    sm2[:], gsl, 0.0, 0.5, OP.is_ge, OP.subtract),
                    reads=[gk], writes=["sm2"])
                p.op("vector", lambda e, gsl=gsl, a3=a3: e.tensor_tensor(
                    gsl.rearrange("p (g t) -> p g t", g=4), gsl.rearrange("p (g t) -> p g t", g=4),
                    a3, OP.mult),
                    reads=["s1bc", gk], writes=[gk])
                p.op("vector", lambda e, gsl=gsl: e.tensor_tensor(smi[:], gsl, sm2[:], OP.add),
                     reads=[gk, "sm2"], writes=["smi"])
                p.op("vector", lambda e, gsl=gsl: e.tensor_tensor(sm4[:], gsl, smi[:], OP.subtract),
                    reads=["smi", gk], writes=["sm4"])
                p.op("vector", lambda e: e.tensor_tensor(
                    sm2[:], sm4[:], ch05[:], OP.is_gt),
                    reads=["sm4", "ch05"], writes=["sm2"])
                p.op("vector", lambda e: e.tensor_tensor(sm2[:], sm2[:], smi[:], OP.add),
                    reads=["sm2", "smi"], writes=["sm2"])
                p.op("vector", lambda e, gidx=gidx: e.scalar_tensor_tensor(
                    nq_sub[:, gidx * 1024:(gidx + 1) * 1024], sm4[:], -0.5, sm2[:],
                    OP.is_lt, OP.subtract),
                    reads=["sm4", "sm2"], writes=["nq"],
                    fence=[f"wqt_{n}{h}" for n in 'qkv' for h in (0, 1)] + ["khiB", "kloB"])
            for qtr in range(4):
                for hh in range(8):
                    hid = qtr * 8 + hh
                    p.op("sync", lambda e, hid=hid, hh=hh: e.dma_start(
                        out=w2qt[:, hh * 1024:(hh + 1) * 1024],
                        in_=w2q_h[:, hid * 128:(hid + 1) * 128], transpose=True),
                        reads=["wqH_w2"], writes=["w2qt"], dma=True,
                        fence=["bh1B", "bh2B"])
                for tt in range(2):
                    pair = (PS[2], PS[3]) if tt == 0 else (PS[4], PS[5])
                    for hh in range(8):
                        for half in range(2):
                            p.op("tensor", lambda e, pair=pair, tt=tt, hh=hh, half=half, qtr=qtr: e.matmul(
                                pair[half][:],
                                nq_sub[:, (qtr * 8 + hh) * 256 + tt * 128:(qtr * 8 + hh) * 256 + tt * 128 + 128],
                                w2qt[:, hh * 1024 + half * 512:hh * 1024 + (half + 1) * 512],
                                start=(qtr == 0 and hh == 0), stop=(qtr == 3 and hh == 7)),
                                reads=["nq", "w2qt"], writes=[pkeys[id(pair[half])]])
            for tt in range(2):
                pair = (PS[2], PS[3]) if tt == 0 else (PS[4], PS[5])
                own0 = cc * CH + sub * 256 + tt * 128
                grid_r = cc * GRID + sub * 256 + tt * 128 + 1
                p.op("sync", lambda e, grid_r=grid_r: e.dma_start(
                    out=evA[:], in_=x_own[grid_r:grid_r + 128, :]),
                    writes=["evA"], dma=True)
                tl = cc * 8 + sub * 2 + tt
                for half in range(2):
                    p.op("vector", lambda e, pair=pair, half=half, tl=tl: e.scalar_tensor_tensor(
                        sm4[:, half * 512:(half + 1) * 512], pair[half][:], s2c[:, tl:tl + 1],
                        evA[:, half * 512:(half + 1) * 512], OP.mult, OP.add),
                        reads=[pkeys[id(pair[half])], "s2c", "evA"], writes=["sm4"])
                p.op("gpsimd", lambda e, own0=own0: e.dma_start(
                    out=part_h[own0:own0 + 128, :], in_=sm4[:]),
                    reads=["sm4"], writes=[f"partH_{own0}"], dma=True)

    part_keys = [f"partH_{o}" for o in range(0, 2 * CH, 128)]

    build_tri_masks()
    qs_hi = BFB[:, 0:1024]
    qs_lo = BFB[:, 1024:2048]
    ks_hi = BFB[:, 2048:3072]
    ks_lo = BFB[:, 3072:4096]
    vt_hi = BFB[:, 4096:5152]
    vt_lo = BFB[:, 5152:6208]
    pT_hi = BFA[:, 0:8192]
    pT_lo = BFA[:, 8192:16384]
    O_loc = FA[:, 0:1056]
    O_acc = {0: FA[:, 1056:2112], 1: FA[:, 2112:3168]}
    pf = evA

    import bass_rust as _br2
    slot_rq = []
    for s in range(NBLK):
        hq = nc.gpsimd.alloc_register(f"rq{s}")
        slot_rq.append((hq, _br2.RuntimeValue(_br2.RegisterHandles([hq]))))

    def load_slot_regs(e):
        last = None
        for s in range(NBLK):
            insts = nc.regs_load(_br2.RegisterHandles([slot_rq[s][0]]), sel_sb[0:1, 18 + s:19 + s])
            if insts:
                last = insts[-1]
        return last
    p.op("gpsimd", load_slot_regs, reads=["sel_sb"], writes=["slotregs"])

    p.op("gpsimd", lambda e: e.memset(mAB[:, 0:16], NEG), writes=["mAB"])
    p.op("gpsimd", lambda e: e.memset(O_loc[:], 0.0), writes=["FA"],
         fence=[f"gq_{i}" for i in range(8)])
    p.op("gpsimd", lambda e: e.memset(O_acc[0][:], 0.0), writes=["FA"])
    p.op("gpsimd", lambda e: e.memset(O_acc[1][:], 0.0), writes=["FA"])

    bfb_fence = ["nq", "w2qt"]
    bfa_fence = list(w1t_keys)

    def slot_loads(s, need_v):
        rq = slot_rq[s][1]
        p.op("gpsimd", lambda e, rq=rq: e.dma_start(out=qs_hi, in_=qall_h[0][:, DS(rq, CH)]),
             reads=qall_done + ["slotregs"], writes=["qs_hi"], dma=True, fence=bfb_fence)
        p.op("gpsimd", lambda e, rq=rq: e.dma_start(out=qs_lo, in_=qall_h[1][:, DS(rq, CH)]),
             reads=qall_done + ["slotregs"], writes=["qs_lo"], dma=True, fence=bfb_fence)
        p.op("sync", lambda e, s=s: e.dma_start(out=ks_hi, in_=kfm_h[0, :, s * CH:(s + 1) * CH]),
             reads=kv_keys, writes=["ks_hi"], dma=True, fence=bfb_fence)
        p.op("sync", lambda e, s=s: e.dma_start(out=ks_lo, in_=kfm_h[1, :, s * CH:(s + 1) * CH]),
             reads=kv_keys, writes=["ks_lo"], dma=True, fence=bfb_fence)
        if need_v:
            for hl, dst, dk in ((0, vt_hi, "vt_hi"), (1, vt_lo, "vt_lo")):
                p.op("sync", lambda e, hl=hl, dst=dst, s=s: e.dma_start(
                    out=dst.rearrange("p (i c) -> p i c", i=8),
                    in_=vtm_h[hl, :, :].rearrange("(i p) c -> p i c", p=128)[:, s * 8:(s + 1) * 8]),
                    reads=vt_keys, writes=[dk], dma=True, fence=bfb_fence)

    for s in range(NBLK):
        slot_loads(s, need_v=False)
        for qt in range(8):
            pair = (PS[0], PS[1]) if qt % 2 == 0 else (PS[2], PS[3])
            kvar = ((qs_hi, "qs_hi", ks_hi, "ks_hi"),
                    (qs_lo, "qs_lo", ks_hi, "ks_hi"),
                    (qs_hi, "qs_hi", ks_lo, "ks_lo"))
            for i, (qb, qk, kb, kk) in enumerate(kvar):
                for hh in range(2):
                    p.op("tensor", lambda e, pair=pair, qb=qb, kb=kb, qt=qt, i=i, hh=hh: e.matmul(
                        pair[hh][:], qb[:, qt * 128:(qt + 1) * 128],
                        kb[:, hh * 512:(hh + 1) * 512], start=(i == 0), stop=(i == 2)),
                        reads=[qk, kk], writes=[pkeys[id(pair[hh])]])
            for hh in range(2):
                if s >= 7:
                    p.op("vector", lambda e, pair=pair, qt=qt, hh=hh: e.tensor_tensor(
                        pair[hh][:], pair[hh][:],
                        tri_qk[:, (8 - qt) * 128 + hh * 512:(8 - qt) * 128 + (hh + 1) * 512], OP.add),
                        reads=[pkeys[id(pair[hh])], "tri_qk"], writes=[pkeys[id(pair[hh])]])
                p.op("vector", lambda e, pair=pair, hh=hh: e.tensor_reduce(
                    colsc[:, 9 + hh:10 + hh], pair[hh][:], AX, OP.max),
                    reads=[pkeys[id(pair[hh])]], writes=["colsc9A"])
            p.op("vector", lambda e, qt=qt: e.tensor_tensor(
                mloc[:, qt:qt + 1], colsc[:, 9:10], colsc[:, 10:11], OP.max),
                reads=["colsc9A"], writes=["mloc"])
        for side in range(2):
            p.op("vector", lambda e, s=s, side=side: e.scalar_tensor_tensor(
                nmst[:], mloc[:, 0:8], selw[:, side * 9 + s:side * 9 + s + 1],
                selw[:, 18 + side * 9 + s:18 + side * 9 + s + 1].broadcast_to([128, 8]),
                OP.mult, OP.add),
                reads=["mloc", "selw"], writes=["nmst"])
            p.op("vector", lambda e, side=side: e.tensor_tensor(
                mAB[:, side * 8:(side + 1) * 8], mAB[:, side * 8:(side + 1) * 8], nmst[:], OP.max),
                reads=["nmst", "mAB"], writes=["mAB"])

    p.op("vector", lambda e: e.tensor_scalar(mAB[:, 16:32], mAB[:, 0:16], -1.0, None, OP.mult),
         reads=["mAB"], writes=["mAB"])

    for s in range(NBLK):
        slot_loads(s, need_v=True)
        p.op("vector", lambda e, s=s: e.tensor_scalar(
            nmst[:], mAB[:, 16:24], selw[:, s:s + 1], None, OP.mult),
            reads=["mAB", "selw"], writes=["nmst"])
        p.op("vector", lambda e, s=s: e.scalar_tensor_tensor(
            nmst[:], mAB[:, 24:32], selw[:, 9 + s:10 + s], nmst[:], OP.mult, OP.add),
            reads=["mAB", "selw", "nmst"], writes=["nmst"])
        for qt in range(8):
            p.op("vector", lambda e, qt=qt: e.tensor_tensor(
                diagS[:, qt * 128:(qt + 1) * 128], identb[:],
                nmst[:, qt:qt + 1].broadcast_to([128, 128]), OP.mult),
                reads=["nmst", "identb"], writes=["xqf"])
        for kc in range(8):
            pair = (PS[0], PS[1]) if kc % 2 == 0 else (PS[2], PS[3])
            kvar = ((ks_hi, "ks_hi", qs_hi, "qs_hi"),
                    (ks_lo, "ks_lo", qs_hi, "qs_hi"),
                    (ks_hi, "ks_hi", qs_lo, "qs_lo"))
            for i, (kb, kk, qb, qk) in enumerate(kvar):
                for hh in range(2):
                    p.op("tensor", lambda e, pair=pair, kb=kb, qb=qb, kc=kc, i=i, hh=hh: e.matmul(
                        pair[hh][:], kb[:, kc * 128:(kc + 1) * 128],
                        qb[:, hh * 512:(hh + 1) * 512], start=(i == 0), stop=False),
                        reads=[kk, qk], writes=[pkeys[id(pair[hh])]])
            for hh in range(2):
                p.op("tensor", lambda e, pair=pair, hh=hh: e.matmul(
                    pair[hh][:], ones_b[:], diagS[:, hh * 512:(hh + 1) * 512],
                    start=False, stop=True),
                    reads=["xqf", "ones_b"], writes=[pkeys[id(pair[hh])]])
                if s >= 7:
                    p.op("vector", lambda e, pair=pair, kc=kc, hh=hh: e.tensor_tensor(
                        pair[hh][:], pair[hh][:],
                        tri_T[:, (8 - kc) * 128 + hh * 512:(8 - kc) * 128 + (hh + 1) * 512], OP.add),
                        reads=[pkeys[id(pair[hh])], "tri_T"], writes=[pkeys[id(pair[hh])]])
                p.op("scalar", lambda e, pair=pair, hh=hh: e.activation(
                    pf[:, hh * 512:(hh + 1) * 512], pair[hh][:], AF.Exp, scale=SCALE),
                    reads=[pkeys[id(pair[hh])]], writes=["evA"])
            p.op("vector", lambda e, kc=kc: e.tensor_copy(pT_hi[:, kc * 1024:(kc + 1) * 1024], pf[:]),
                 reads=["evA"], writes=["pT_hi"], fence=bfa_fence)
            p.op("vector", lambda e, kc=kc: e.scalar_tensor_tensor(
                pT_lo[:, kc * 1024:(kc + 1) * 1024], pT_hi[:, kc * 1024:(kc + 1) * 1024], -1.0,
                pf[:], OP.mult, OP.add),
                reads=["pT_hi", "evA"], writes=["pT_lo"], fence=bfa_fence)
        for qt in range(8):
            pv = PS[4] if qt % 2 == 0 else PS[5]
            pvk = pkeys[id(pv)]
            nmm = 0
            for kc in range(8):
                for (pb, pk, vb, vk) in ((pT_hi, "pT_hi", vt_hi, "vt_hi"),
                                         (pT_hi, "pT_hi", vt_lo, "vt_lo"),
                                         (pT_lo, "pT_lo", vt_hi, "vt_hi")):
                    p.op("tensor", lambda e, pv=pv, pb=pb, vb=vb, kc=kc, qt=qt,
                         st=(nmm == 0), sp=(nmm == 23): e.matmul(
                        pv[:, 0:132], pb[:, kc * 1024 + qt * 128:kc * 1024 + (qt + 1) * 128],
                        vb[:, kc * 132:(kc + 1) * 132], start=st, stop=sp),
                        reads=[pk, vk], writes=[pvk])
                    nmm += 1
            p.op("vector", lambda e, pv=pv, qt=qt: e.tensor_tensor(
                O_loc[:, qt * 132:(qt + 1) * 132], pv[:, 0:132],
                O_loc[:, qt * 132:(qt + 1) * 132], OP.add),
                reads=[pvk, "FA"], writes=["FA"])
        for side in range(2):
            p.op("vector", lambda e, s=s, side=side: e.scalar_tensor_tensor(
                O_acc[side][:], O_loc[:], selw[:, side * 9 + s:side * 9 + s + 1],
                O_acc[side][:], OP.mult, OP.add),
                reads=["FA", "selw"], writes=["FA"])
        p.op("gpsimd", lambda e: e.memset(O_loc[:, 0:1056], 0.0), reads=["FA"], writes=["FA"])

    for g in range(16):
        Osl = O_acc[g // 8][:, (g % 8) * 132:(g % 8) * 132 + 132]
        p.op("vector", lambda e, Osl=Osl: e.reciprocal(colsc[:, 12:13], Osl[:, 128:129]),
             reads=["FA"], writes=["colsc12"])
        p.op("vector", lambda e, Osl=Osl: e.tensor_scalar(
            sm1[:, 0:128], Osl[:, 0:128], colsc[:, 12:13], None, OP.mult),
            reads=["FA", "colsc12"], writes=["sm1"])
        p.op("tensor", lambda e: e.transpose(psT[:, 0:128], sm1[:, 0:128], identf[:]),
             reads=["sm1", "identf"], writes=["PSb7a"])
        p.op("scalar", lambda e: e.activation(bh1[:, 0:128], psT[:, 0:128], AF.Copy),
             reads=["PSb7a"], writes=["bh1"])
        p.op("vector", lambda e: e.scalar_tensor_tensor(
            bh2[:, 0:128], bh1[:, 0:128], -1.0, psT[:, 0:128], OP.mult, OP.add),
            reads=["bh1", "PSb7a"], writes=["bh2"])
        mmspec = ((bh1, "bh1", wot[0], "wot0"),
                  (bh1, "bh1", wot[1], "wot1"),
                  (bh2, "bh2", wot[0], "wot0"))
        for i, (ob, obk, wt, wtk) in enumerate(mmspec):
            for hh in range(2):
                p.op("tensor", lambda e, ob=ob, wt=wt, i=i, hh=hh: e.matmul(
                    PS[hh][:], ob[:, 0:128], wt[:, hh * 512:(hh + 1) * 512],
                    start=(i == 0), stop=False),
                    reads=[obk, wtk], writes=[pkeys[id(PS[hh])]])
        for hh in range(2):
            p.op("tensor", lambda e, hh=hh: e.matmul(
                PS[hh][:], ones_f[0:1, :], bo_row[:, hh * 512:(hh + 1) * 512],
                start=False, stop=True),
                reads=["bo_row", "ones_f"], writes=[pkeys[id(PS[hh])]])
        p.op("sync", lambda e, g=g: e.dma_start(out=evA[:], in_=part_h[g * 128:(g + 1) * 128, :]),
             reads=part_keys, writes=["evA"], dma=True)
        for hh in range(2):
            p.op("vector", lambda e, hh=hh: e.tensor_tensor(
                sm4[:, hh * 512:(hh + 1) * 512], PS[hh][:], evA[:, hh * 512:(hh + 1) * 512], OP.add),
                reads=[pkeys[id(PS[hh])], "evA"], writes=["sm4"])
        p.op("gpsimd", lambda e, g=g: e.dma_start(out=out_d[g * 128:(g + 1) * 128, :], in_=sm4[:]),
             reads=["sm4"], writes=[f"outH_{g}"], dma=True)

    with nc.Block() as block:
        p.emit(block)
    return nc


_NC_CACHE = None


def _get_nc():
    global _NC_CACHE
    if _NC_CACHE is None:
        _NC_CACHE = build_program()
    return _NC_CACHE


def _make_inmaps(x, wq, wk, wv, wo, bq, bk, bv, bo, w1, w2, cw, cb):
    in_maps = []
    for j in range(8):
        slots = [(0, bb) for bb in range(j)] + [(1, bb) for bb in range(7 - j)]
        slots += [(0, j), (1, 7 - j)]
        x_kv = np.concatenate([x[bat, bb * CH:(bb + 1) * CH] for bat, bb in slots], axis=0)
        x_kv = np.ascontiguousarray(x_kv)
        x_own = np.zeros((OWNP, D), np.float32)
        for cc, (bb, c0) in enumerate(((0, CH * j), (1, CH * (7 - j)))):
            lo, hi = c0 - 1, c0 + CH + 1
            slo, shi = max(lo, 0), min(hi, S)
            x_own[cc * GRID + (slo - lo): cc * GRID + (shi - lo)] = x[bb, slo:shi]
        sel = np.zeros((1, 64), np.int32)
        selw = np.zeros((128, 36), np.float32)
        selw[:, 18:36] = NEG
        for s, (bat, bb) in enumerate(slots):
            sel[0, 18 + s] = 0 if bat == 0 else CH
            selw[:, (0 if bat == 0 else 9) + s] = 1.0
            selw[:, 18 + (0 if bat == 0 else 9) + s] = 0.0
        in_maps.append(dict(
            x_kv=x_kv, x_own=x_own, wq=wq, wk=wk, wv=wv, wo=wo, bq=bq, bk=bk, bv=bv,
            bo=bo, w1=w1, w2=w2, cw=cw, cb=cb, seltab=sel, selw_in=selw))
    return in_maps


def kernel(L0_attn_x, L0_attn_Wq, L0_attn_bq, L0_attn_Wk, L0_attn_bk, L0_attn_Wv,
           L0_attn_bv, L0_attn_Wo, L0_attn_bo, L0_ffn_w1, L0_ffn_conv_w, L0_ffn_conv_b,
           L0_ffn_w2):
    x = np.ascontiguousarray(np.asarray(L0_attn_x, dtype=np.float32))
    wq = np.ascontiguousarray(np.asarray(L0_attn_Wq, dtype=np.float32))
    wk = np.ascontiguousarray(np.asarray(L0_attn_Wk, dtype=np.float32))
    wv = np.ascontiguousarray(np.asarray(L0_attn_Wv, dtype=np.float32))
    wo = np.ascontiguousarray(np.asarray(L0_attn_Wo, dtype=np.float32))
    bq = np.asarray(L0_attn_bq, dtype=np.float32).reshape(1, H)
    bk = np.asarray(L0_attn_bk, dtype=np.float32).reshape(1, H)
    bv = np.asarray(L0_attn_bv, dtype=np.float32).reshape(1, H)
    bo = np.asarray(L0_attn_bo, dtype=np.float32).reshape(1, D)
    w1 = np.ascontiguousarray(np.asarray(L0_ffn_w1, dtype=np.float32))
    w2 = np.ascontiguousarray(np.asarray(L0_ffn_w2, dtype=np.float32))
    cw3 = np.asarray(L0_ffn_conv_w, dtype=np.float32)[:, 0, :]
    cw = np.ascontiguousarray(
        cw3.reshape(32, 128, 3).transpose(1, 2, 0).reshape(128, 96))
    cb = np.ascontiguousarray(
        np.asarray(L0_ffn_conv_b, dtype=np.float32).reshape(32, 128).T)

    in_maps = _make_inmaps(x, wq, wk, wv, wo, bq, bk, bv, bo, w1, w2, cw, cb)
    nc = _get_nc()
    res = run_bass_kernel_spmd(nc, in_maps, core_ids=list(range(8)))
    out = np.empty((B, S, D), np.float32)
    for j in range(8):
        o = np.asarray(res.results[j]["out_own"])
        out[0, CH * j:CH * (j + 1)] = o[0:CH]
        out[1, CH * (7 - j):CH * (8 - j)] = o[CH:2 * CH]
    return out


# version 14
# speedup vs baseline: 1.0012x; 1.0012x over previous
import numpy as np
import ml_dtypes
import concourse.bass as bass
import concourse.mybir as mybir
from concourse.bass_utils import run_bass_kernel_spmd

F32 = mybir.dt.float32
BF16 = mybir.dt.bfloat16
I32 = mybir.dt.int32
AX = mybir.AxisListType.X
OP = mybir.AluOpType
AF = mybir.ActivationFunctionType
ET = mybir.EngineType
DS = bass.ds

B, S, D, H, DH = 2, 8192, 1024, 128, 4096
CH = 1024
KV = 9216
NBLK = KV // CH
GRID = 1152
OWNP = 2 * GRID
SCALE = float(np.float32(128.0 ** -0.5))
NEG = -1.0e30

ENGINES = ("tensor", "vector", "scalar", "gpsimd", "sync")


class Prog:

    def __init__(self, nc):
        self.nc = nc
        self.ops = []
        self.last_writer = {}
        self.readers = {}

    def op(self, engine, fn, reads=(), writes=(), dma=False, fence=()):
        i = len(self.ops)
        deps = set()
        for b in reads:
            if b in self.last_writer:
                deps.add(self.last_writer[b])
        for b in list(writes) + list(fence):
            if b in self.last_writer:
                deps.add(self.last_writer[b])
            for r in self.readers.get(b, ()):
                deps.add(r)
        deps.discard(i)
        self.ops.append((engine, fn, deps, dma))
        for b in writes:
            self.last_writer[b] = i
            self.readers[b] = []
        for b in reads:
            self.readers.setdefault(b, []).append(i)
        return i

    def emit(self, block):
        nc = self.nc
        n = len(self.ops)
        needed = [False] * n
        for (_, _, deps, _) in self.ops:
            for d in deps:
                needed[d] = True
        NROT = 8
        sems = {}
        for e in ENGINES:
            sems[(e, False)] = nc.alloc_semaphore(f"s_{e}_c")
            for r in range(NROT):
                sems[(e, True, r)] = nc.alloc_semaphore(f"s_{e}_d{r}")
        count = [0] * n
        slot = [None] * n
        cur_c = {e: 0 for e in ENGINES}
        cur_d = {(e, r): 0 for e in ENGINES for r in range(NROT)}
        nseen_d = {e: 0 for e in ENGINES}
        for i, (eng, fn, deps, isd) in enumerate(self.ops):
            if isd:
                r = nseen_d[eng] % NROT
                nseen_d[eng] += 1
                slot[i] = r
                cur_d[(eng, r)] += 16
                count[i] = cur_d[(eng, r)]
            elif needed[i]:
                cur_c[eng] += 1
                count[i] = cur_c[eng]
            else:
                count[i] = -1
        final_c = dict(cur_c)
        final_d = dict(cur_d)
        streams = {e: [] for e in ENGINES}
        waited = {e: {} for e in ENGINES}

        def add_wait(waits, eng, key, c):
            if waited[eng].get(key, 0) >= c:
                return
            waits.append((key, c))
            waited[eng][key] = c

        for i, (eng, fn, deps, isd) in enumerate(self.ops):
            waits = []
            if isd:
                key = (eng, True, slot[i])
                prior = count[i] - 16
                if prior > 0:
                    add_wait(waits, eng, key, prior)
            for d in sorted(deps):
                peng, pisd = self.ops[d][0], self.ops[d][3]
                c = count[d]
                if c < 0:
                    c = None
                    for k in range(d + 1, n):
                        if (self.ops[k][0], self.ops[k][3]) == (peng, pisd) and not pisd                                 and count[k] > 0:
                            c = count[k]
                            break
                    if c is None:
                        continue
                key = (peng, True, slot[d]) if pisd else (peng, False)
                add_wait(waits, eng, key, c)
            streams[eng].append((waits, fn, count[i] if count[i] > 0 else 0, isd,
                                 slot[i]))

        def make_body(e_name):
            def body(engine_obj):
                for waits, fn, cnt, isd, sl in streams[e_name]:
                    for key, c in waits:
                        engine_obj.wait_ge(sems[key], c)
                    inst = fn(engine_obj)
                    if inst is not None:
                        if isd:
                            inst.then_inc(sems[(e_name, True, sl)], 16)
                        elif cnt > 0:
                            inst.then_inc(sems[(e_name, False)], 1)
                for r in range(NROT):
                    if final_d[(e_name, r)] > 0:
                        engine_obj.wait_ge(sems[(e_name, True, r)], final_d[(e_name, r)])
            return body

        block.tensor(make_body("tensor"))
        block.vector(make_body("vector"))
        block.scalar(make_body("scalar"))
        block.gpsimd(make_body("gpsimd"))
        block.sync(make_body("sync"))


def build_program():
    nc = bass.Bass("TRN2", target_bir_lowering=False)
    p = Prog(nc)
    import bass_rust as _br

    def _vload(ap):
        v = nc.values_load(ap, engines=[ET.Pool], skip_runtime_bounds_check=True)
        return v

    def _vfree(v):
        nc.gpsimd.free_register(list(v.val)[0])

    x_kv = nc.dram_tensor("x_kv", [KV, D], F32, kind="ExternalInput").ap()
    x_own = nc.dram_tensor("x_own", [OWNP, D], F32, kind="ExternalInput").ap()
    wq_d = nc.dram_tensor("wq", [H, D], F32, kind="ExternalInput").ap()
    wk_d = nc.dram_tensor("wk", [H, D], F32, kind="ExternalInput").ap()
    wv_d = nc.dram_tensor("wv", [H, D], F32, kind="ExternalInput").ap()
    wo_d = nc.dram_tensor("wo", [D, H], F32, kind="ExternalInput").ap()
    bq_d = nc.dram_tensor("bq", [1, H], F32, kind="ExternalInput").ap()
    bk_d = nc.dram_tensor("bk", [1, H], F32, kind="ExternalInput").ap()
    bv_d = nc.dram_tensor("bv", [1, H], F32, kind="ExternalInput").ap()
    bo_d = nc.dram_tensor("bo", [1, D], F32, kind="ExternalInput").ap()
    w1_d = nc.dram_tensor("w1", [DH, D], F32, kind="ExternalInput").ap()
    w2_d = nc.dram_tensor("w2", [D, DH], F32, kind="ExternalInput").ap()
    cw_d = nc.dram_tensor("cw", [128, 96], F32, kind="ExternalInput").ap()
    cb_d = nc.dram_tensor("cb", [128, 32], F32, kind="ExternalInput").ap()
    sel_d = nc.dram_tensor("seltab", [1, 64], I32, kind="ExternalInput").ap()
    selw_d = nc.dram_tensor("selw_in", [128, 36], F32, kind="ExternalInput").ap()
    out_d = nc.dram_tensor("out_own", [2 * CH, D], F32, kind="ExternalOutput").ap()

    xq1_h = nc.dram_tensor("xq1_h", [OWNP, D], BF16).ap()
    w1q_h = nc.dram_tensor("w1q_h", [DH, D], BF16).ap()
    w2q_h = nc.dram_tensor("w2q_h", [D, DH], BF16).ap()
    kfm_h = nc.dram_tensor("kfm_h", [2, H, KV], BF16).ap()
    vtm_h = nc.dram_tensor("vtm_h", [2, 72 * H, 132], BF16).ap()
    qall_h = {hl: nc.dram_tensor(f"qall{hl}", [H, 2 * CH], BF16).ap() for hl in (0, 1)}
    part_h = nc.dram_tensor("part_h", [2 * CH, D], F32).ap()

    sb = lambda name, shape, dt=F32: nc.alloc_sbuf_tensor(name, shape, dt)
    ones_f = sb("ones_f", [128, 128])
    ones_b = sb("ones_b", [128, 128], BF16)
    identf = sb("identf", [128, 128])
    identb = sb("identb", [128, 128], BF16)
    tri_T = sb("tri_T", [128, 16 * 128], BF16)
    tri_qk = sb("tri_qk", [128, 16 * 128], BF16)
    wot = {0: sb("wot0", [128, D], BF16), 1: sb("wot1", [128, D], BF16)}
    brow = {nm: sb(f"brow_{nm}", [1, 128]) for nm in ("bq", "bk", "bv")}
    bo_row = sb("bo_row", [1, D])
    cw_sb = sb("cw_sb", [128, 3 * 32])
    cb_sb = sb("cb_sb", [128, 32])
    wsc = sb("wsc", [128, 4])
    sel_sb = nc.alloc_sbuf_tensor("sel_sb", [1, 64], I32)
    s1c = sb("s1c", [128, 18])
    s2c = sb("s2c", [128, 16])
    mAB = sb("mAB", [128, 32])
    selw = sb("selw", [128, 36])
    BFA = sb("BFA", [128, 32768], BF16)
    BFB = sb("BFB", [128, 16384], BF16)
    wqt = {}
    for ii, (nm, hl) in enumerate([(n, h) for n in "qkv" for h in (0, 1)]):
        wqt[(nm, hl)] = BFB[:, ii * 1024:(ii + 1) * 1024]
    FA = sb("FA", [128, 8192], F32)
    trash = tri_T[:, 0:1024]
    xnf = sb("xnf", [128, 1024])
    sm1 = sb("sm1", [128, 1024])
    sm2 = sb("sm2", [128, 1024])
    sm4 = sb("sm4", [128, 1024])
    smi = nc.alloc_sbuf_tensor("smi", [128, 1024], I32)
    evA = sb("evA", [128, 1024])
    bh1 = sb("bh1", [128, 1024], BF16)
    bh2 = sb("bh2", [128, 1024], BF16)
    khi_b = BFB[:, 6144:7168]
    klo_b = BFB[:, 7168:8192]
    bh1b = BFB[:, 8192:9216]
    bh2b = BFB[:, 9216:10240]
    xqf = sb("xqf", [128, 8 * 272], BF16)
    hts = [sb("hts0", [128, 258]), sb("hts1", [128, 258])]
    cv1 = [sb("cv1_0", [128, 256]), sb("cv1_1", [128, 256])]
    cv2 = [sb("cv2_0", [128, 256]), sb("cv2_1", [128, 256])]
    s1bc = sb("s1bc", [128, 384])
    ascbc = s1bc
    M1 = [sb("M1_0", [128, 256]), sb("M1_1", [128, 256])]
    M1n = [sb("M1n_0", [128, 256]), sb("M1n_1", [128, 256])]
    diagt = sb("diagt", [128, 128])
    diagS = xqf[:, 0:1024]
    vtt = {0: sb("vtt0", [128, 132], BF16), 1: sb("vtt1", [128, 132], BF16)}
    nmst = sb("nmst", [128, 8])
    mloc = sb("mloc", [128, 8])
    colsc = sb("colsc", [128, 24])
    ch05 = sb("ch05", [128, 1024])
    dsc = sb("dsc", [128, 64])
    PS = [nc.alloc_psum_tensor(f"PS{i}", [128, 512], F32) for i in range(8)]
    psT = PS[7]

    def mm512(pair, lhsT, rhs1024, start, stop, reads, wkeys):
        for h in range(2):
            p.op("tensor", lambda e, h=h, lhsT=lhsT, rhs1024=rhs1024, start=start, stop=stop: e.matmul(
                pair[h][:], lhsT, rhs1024[:, h * 512:(h + 1) * 512], start=start, stop=stop),
                reads=list(reads), writes=[wkeys[h]])

    p.op("gpsimd", lambda e: e.memset(ones_f[:], 1.0), writes=["ones_f"])
    p.op("gpsimd", lambda e: e.memset(ones512[:], 1.0), writes=["ones512"])
    p.op("gpsimd", lambda e: e.memset(ones_b[:], 1.0), writes=["ones_b"])
    p.op("gpsimd", lambda e: e.affine_select(identf[:], ones_f[:], [[-1, 128]], OP.is_equal,
                                             0.0, channel_multiplier=1),
         reads=["ones_f"], writes=["identf"])
    p.op("vector", lambda e: e.tensor_copy(identb[:], identf[:]), reads=["identf"], writes=["identb"])
    def build_tri_masks():
        p.op("gpsimd", lambda e: e.memset(tri_T[:], 0.0), writes=["tri_T"], fence=["trash"])
        p.op("gpsimd", lambda e: e.memset(tri_qk[:], 0.0), writes=["tri_qk"])
        p.op("gpsimd", lambda e: e.affine_select(tri_T.rearrange("p (c q) -> p c q", c=16),
                                                 tri_T.rearrange("p (c q) -> p c q", c=16),
                                                 [[128, 16], [1, 128]], OP.is_ge, NEG,
                                                 base=-8 * 128, channel_multiplier=-1),
             reads=["tri_T"], writes=["tri_T"])
        p.op("gpsimd", lambda e: e.affine_select(tri_qk.rearrange("p (c k) -> p c k", c=16),
                                                 tri_qk.rearrange("p (c k) -> p c k", c=16),
                                                 [[-128, 16], [-1, 128]], OP.is_ge, NEG,
                                                 base=8 * 128, channel_multiplier=1),
             reads=["tri_qk"], writes=["tri_qk"])
    p.op("sync", lambda e: e.dma_start(out=sel_sb[:], in_=sel_d), writes=["sel_sb"], dma=True)
    p.op("sync", lambda e: e.dma_start(out=selw[:], in_=selw_d), writes=["selw"], dma=True)
    p.op("sync", lambda e: e.dma_start(out=bo_row[:], in_=bo_d), writes=["bo_row"], dma=True)
    for nm, t in (("bq", bq_d), ("bk", bk_d), ("bv", bv_d)):
        p.op("sync", lambda e, t=t, d=brow[nm]: e.dma_start(out=d[:], in_=t),
             writes=[nm], dma=True)
    p.op("sync", lambda e: e.dma_start(out=cw_sb[:], in_=cw_d), writes=["cw_sb"], dma=True)
    p.op("sync", lambda e: e.dma_start(out=cb_sb[:], in_=cb_d), writes=["cb_sb"], dma=True)

    wtmp, whib, wlob = ch05, bh1, bh2

    def small_w_transpose(srcfn, srckey, dsts, dstkeys):
        for c in range(8):
            p.op("tensor", lambda e, c=c: e.transpose(
                psT[:, 0:128], wtmp[:, c * 128:(c + 1) * 128], identf[:]),
                reads=[srckey, "identf"], writes=["PSb7a"])
            p.op("scalar", lambda e, d=dsts[0], c=c: e.activation(
                d[:, c * 128:(c + 1) * 128], psT[:, 0:128], AF.Copy),
                reads=["PSb7a"], writes=[dstkeys[0]])
            p.op("vector", lambda e, d0=dsts[0], d1=dsts[1], c=c: e.scalar_tensor_tensor(
                d1[:, c * 128:(c + 1) * 128], d0[:, c * 128:(c + 1) * 128], -1.0,
                psT[:, 0:128], OP.mult, OP.add),
                reads=["PSb7a", dstkeys[0]], writes=[dstkeys[1]])

    for nm, t in (("q", wq_d), ("k", wk_d), ("v", wv_d)):
        p.op("sync", lambda e, t=t: e.dma_start(out=wtmp[:], in_=t), writes=["ch05"], dma=True)
        small_w_transpose(None, "ch05", {0: wqt[(nm, 0)], 1: wqt[(nm, 1)]},
                          {0: f"wqt_{nm}0", 1: f"wqt_{nm}1"})
    p.op("sync", lambda e: e.dma_start(out=wtmp.rearrange("p (c h) -> p c h", c=8),
                                       in_=wo_d.rearrange("(c p) h -> p c h", p=128)),
         writes=["ch05"], dma=True)
    small_w_transpose(None, "ch05", wot, {0: "wot0", 1: "wot1"})
    p.op("gpsimd", lambda e: e.memset(ch05[:], 0.5), writes=["ch05"])

    qbuf = [sb("qbufA", [128, 512]), sb("qbufB", [128, 512])]
    ones512 = sb("ones512", [1, 512])
    qge = sb("qge", [128, 512], BF16)
    qle = sb("qle", [128, 512], BF16)
    qsub = [sb("qsubA", [128, 512], BF16), sb("qsubB", [128, 512], BF16)]

    def mk_quant_units(wsrc, n_rows, n_cols, dst_h, sc_col, tag):
        ntile = n_rows // 128
        nch = n_cols // 512
        units = []
        units.append(lambda: p.op("vector", lambda e: e.memset(colsc[:, 0:1], 0.0),
                                  writes=["colsc0"], fence=["colsc1"]))
        for i, (t0, ch) in enumerate([(t, c) for t in range(ntile) for c in range(nch)]):
            def em(t0=t0, ch=ch, par=i % 2, q="sync"):
                b = qbuf[par]
                p.op(q, lambda e, t0=t0, ch=ch, b=b: e.dma_start(
                    out=b[:], in_=wsrc[t0 * 128:(t0 + 1) * 128, ch * 512:(ch + 1) * 512]),
                    writes=[f"qbuf{par}"], dma=True)
                p.op("vector", lambda e, b=b: e.tensor_reduce(
                    colsc[:, 1:2], b[:], AX, OP.add, apply_absolute_value=True),
                    reads=[f"qbuf{par}"], writes=["colsc1"])
                p.op("vector", lambda e: e.tensor_tensor(
                    colsc[:, 0:1], colsc[:, 0:1], colsc[:, 1:2], OP.add),
                    reads=["colsc1", "colsc0"], writes=["colsc0"])
            units.append(em)

        def fin():
            p.op("tensor", lambda e: e.matmul(PS[6][0:1, 0:1], ones_f[:, 0:1], colsc[:, 0:1],
                                              start=True, stop=True),
                 reads=["colsc0", "ones_f"], writes=["PSb6"])
            p.op("vector", lambda e, nn=float(n_rows * n_cols): e.tensor_scalar(
                colsc[0:1, 2:3], PS[6][0:1, 0:1], 1.0 / nn, 1e-5, OP.mult, OP.max),
                reads=["PSb6"], writes=["colsc2"])
            p.op("vector", lambda e: e.reciprocal(colsc[0:1, 3:4], colsc[0:1, 2:3]),
                 reads=["colsc2"], writes=["colsc3"])
            p.op("tensor", lambda e: e.matmul(PS[6][:, 1:2], ones_f[0:1, :], colsc[0:1, 3:4],
                                              start=True, stop=True),
                 reads=["colsc3", "ones_f"], writes=["PSb6"])
            p.op("tensor", lambda e: e.matmul(PS[6][:, 2:3], ones_f[0:1, :], colsc[0:1, 2:3],
                                              start=True, stop=True),
                 reads=["colsc2", "ones_f"], writes=["PSb6"])
            p.op("vector", lambda e, sc=sc_col: e.tensor_copy(wsc[:, sc:sc + 1], PS[6][:, 1:2]),
                 reads=["PSb6"], writes=[f"wsc{sc_col}"])
            p.op("vector", lambda e, sc=sc_col: e.tensor_copy(wsc[:, sc + 2:sc + 3], PS[6][:, 2:3]),
                 reads=["PSb6"], writes=[f"wsc{sc_col + 2}"])
        units.append(fin)
        for i, (t0, ch) in enumerate([(t, c) for t in range(ntile) for c in range(nch)]):
            def em2(t0=t0, ch=ch, par=i % 2, q="sync"):
                b = qbuf[par]
                p.op(q, lambda e, t0=t0, ch=ch, b=b: e.dma_start(
                    out=b[:], in_=wsrc[t0 * 128:(t0 + 1) * 128, ch * 512:(ch + 1) * 512]),
                    writes=[f"qbuf{par}"], dma=True)
                p.op("vector", lambda e, b=b, sc=sc_col: e.tensor_scalar(
                    qge[:], b[:], wsc[:, sc:sc + 1], 0.5, OP.mult, OP.is_ge),
                    reads=[f"qbuf{par}", f"wsc{sc_col}"], writes=["qge"])
                p.op("vector", lambda e, b=b, sc=sc_col: e.tensor_scalar(
                    qle[:], b[:], wsc[:, sc:sc + 1], -0.5, OP.mult, OP.is_le),
                    reads=[f"qbuf{par}", f"wsc{sc_col}"], writes=["qle"])
                p.op("vector", lambda e, par=par: e.tensor_tensor(
                    qsub[par][:], qle[:], qge[:], OP.subtract),
                    reads=["qge", "qle"], writes=[f"qsub{par}"])
                p.op("sync", lambda e, t0=t0, ch=ch, par=par: e.dma_start(
                    out=dst_h[t0 * 128:(t0 + 1) * 128, ch * 512:(ch + 1) * 512],
                    in_=qsub[par][:]),
                    reads=[f"qsub{par}"], writes=[f"wqH_{tag}"], dma=True)
            units.append(em2)
        return units

    xb8 = [FA[:, i * 1024:(i + 1) * 1024] for i in range(8)]

    def ln_loads(xsrc, t0, half, nt):
        for i in range(nt):
            ti = half * 4 + i
            p.op("gpsimd", lambda e, row=(t0 + i) * 128, ti=ti, xsrc=xsrc: e.dma_start(
                out=xb8[ti], in_=xsrc[row:row + 128, :]),
                writes=[f"lnx_{ti}"], dma=True, fence=["FA", "FA2"])

    def ln_stats4(half, nt):
        K = f"dsc{half}"
        for i in range(nt):
            ti = half * 4 + i
            p.op("scalar", lambda e, ti=ti: e.activation(trash[:], xb8[ti][:], AF.Copy,
                                                         accum_out=dsc[:, ti:ti + 1]),
                 reads=[f"lnx_{ti}"], writes=[K, "trash"])
            p.op("scalar", lambda e, ti=ti: e.activation(trash[:], xb8[ti][:], AF.Square,
                                                         accum_out=dsc[:, 8 + ti:9 + ti]),
                 reads=[f"lnx_{ti}"], writes=[K, "trash"])

    def ln_chain4(half, nt):
        c = dsc
        K = f"dsc{half}"
        b = half * 4
        sm, sv, st = 32 + half * 16, 36 + half * 16, 40 + half * 16

        def v(fn):
            p.op("vector", fn, reads=[K], writes=[K])
        v(lambda e: e.tensor_scalar(c[:, sm:sm + nt], c[:, b:b + nt], 1.0 / D, None, OP.mult))
        v(lambda e: e.tensor_tensor(c[:, sv:sv + nt], c[:, sm:sm + nt], c[:, sm:sm + nt], OP.mult))
        v(lambda e: e.scalar_tensor_tensor(c[:, sv:sv + nt], c[:, 8 + b:8 + b + nt], 1.0 / D,
                                           c[:, sv:sv + nt], OP.mult, OP.subtract))
        v(lambda e: e.tensor_scalar(c[:, sv:sv + nt], c[:, sv:sv + nt], 1e-5, None, OP.add))
        p.op("scalar", lambda e: e.activation(c[:, st:st + nt], c[:, sv:sv + nt], AF.Sqrt),
             reads=[K], writes=[K])
        v(lambda e: e.reciprocal(c[:, 16 + b:16 + b + nt], c[:, st:st + nt]))
        v(lambda e: e.tensor_tensor(c[:, st:st + nt], c[:, 16 + b:16 + b + nt],
                                    c[:, 16 + b:16 + b + nt], OP.mult))
        v(lambda e: e.tensor_tensor(c[:, st:st + nt], c[:, st:st + nt], c[:, sv:sv + nt], OP.mult))
        v(lambda e: e.tensor_scalar(c[:, st:st + nt], c[:, st:st + nt], -0.5, 1.5, OP.mult, OP.add))
        v(lambda e: e.tensor_tensor(c[:, 16 + b:16 + b + nt], c[:, 16 + b:16 + b + nt],
                                    c[:, st:st + nt], OP.mult))
        v(lambda e: e.tensor_tensor(c[:, 24 + b:24 + b + nt], c[:, sm:sm + nt],
                                    c[:, 16 + b:16 + b + nt], OP.mult))
        v(lambda e: e.tensor_scalar(c[:, 24 + b:24 + b + nt], c[:, 24 + b:24 + b + nt],
                                    -1.0, None, OP.mult))

    lnpair = {0: (bh1, bh2, "bh1", "bh2"), 1: (bh1b, bh2b, "bh1B", "bh2B")}
    xnfp = {0: (xnf, "xnf"), 1: (sm1, "sm1")}

    def ln_apply4(ti, half):
        K = f"dsc{half}"
        xb, xk = xnfp[ti % 2]
        p.op("vector", lambda e, ti=ti, xb=xb: e.tensor_scalar(xb[:], xb8[ti][:], dsc[:, 16 + ti:17 + ti],
                                                               dsc[:, 24 + ti:25 + ti], OP.mult, OP.add),
             reads=[f"lnx_{ti}", K], writes=[xk])

    kv_consume_order = [7, 8, 0, 1, 2, 3, 4, 5, 6]
    par_of_block = {b: i % 2 for i, b in enumerate(kv_consume_order)}

    def kv_post(t):
        xb, xk = xnfp[t % 2]
        par = par_of_block[t // 8]
        j = t % 8
        for c in range(8):
            ps = psT[:, 0:128] if c % 2 == 0 else PS[6][:, 0:128]
            psk = "PSb7a" if c % 2 == 0 else "PSb6"
            p.op("tensor", lambda e, c=c, xb=xb, ps=ps: e.transpose(
                ps, xb[:, c * 128:(c + 1) * 128], identf[:]),
                reads=[xk, "identf"], writes=[psk])
            p.op("scalar", lambda e, c=c, ps=ps, par=par, j=j: e.activation(
                xnfm2[par][(c, 0)][:, j * 128:(j + 1) * 128], ps, AF.Copy),
                reads=[psk], writes=[f"xnfm_{par}_{c}_0"])
            p.op("vector", lambda e, c=c, ps=ps, par=par, j=j: e.scalar_tensor_tensor(
                xnfm2[par][(c, 1)][:, j * 128:(j + 1) * 128],
                xnfm2[par][(c, 0)][:, j * 128:(j + 1) * 128], -1.0, ps, OP.mult, OP.add),
                reads=[psk, f"xnfm_{par}_{c}_0"], writes=[f"xnfm_{par}_{c}_1"])

    def own_post(t):
        xb, xk = xnfp[t % 2]
        xnf_l = xb
        p.op("vector", lambda e, xnf=xnf_l: e.tensor_reduce(colsc[:, 14:15], xnf[:], AX, OP.max,
                                                            apply_absolute_value=True),
             reads=[xk], writes=["colscq"])
        p.op("vector", lambda e: e.tensor_scalar(colsc[:, 15:16], colsc[:, 14:15], 1e-5, None, OP.max),
             reads=["colscq"], writes=["colscq"])
        p.op("vector", lambda e: e.reciprocal(colsc[:, 16:17], colsc[:, 15:16]),
             reads=["colscq"], writes=["colscq"])
        p.op("vector", lambda e: e.tensor_scalar(colsc[:, 17:18], colsc[:, 16:17], 127.0, None, OP.mult),
             reads=["colscq"], writes=["colscq"])
        p.op("vector", lambda e, t=t: e.scalar_tensor_tensor(
            s1c[:, t:t + 1], colsc[:, 15:16], 1.0 / 127.0, wsc[:, 2:3], OP.mult, OP.mult),
            reads=["colscq", "wsc2"], writes=["s1c"])
        p.op("vector", lambda e, xnf=xnf_l: e.tensor_scalar(sm4[:], xnf[:], colsc[:, 17:18], None, OP.mult),
             reads=[xk, "colscq"], writes=["sm4"])
        p.op("vector", lambda e, xnf=xnf_l: e.tensor_scalar(sm2[:], xnf[:], 0.0, 0.5, OP.is_ge, OP.subtract),
             reads=[xk], writes=["sm2"])
        p.op("vector", lambda e: e.tensor_tensor(smi[:], sm4[:], sm2[:], OP.add),
             reads=["sm4", "sm2"], writes=["smi"])
        p.op("vector", lambda e: e.tensor_tensor(sm4[:], sm4[:], smi[:], OP.subtract),
             reads=["smi", "sm4"], writes=["sm4"])
        p.op("vector", lambda e: e.tensor_tensor(sm2[:], sm4[:], ch05[:], OP.is_gt),
             reads=["sm4", "ch05"], writes=["sm2"])
        p.op("vector", lambda e: e.tensor_tensor(sm2[:], sm2[:], smi[:], OP.add),
             reads=["sm2", "smi"], writes=["sm2"])
        hb, lb, hk, lk = lnpair[t % 2]
        p.op("vector", lambda e, hb=hb: e.scalar_tensor_tensor(hb[:], sm4[:], -0.5, sm2[:], OP.is_lt, OP.subtract),
             reads=["sm4", "sm2"], writes=[hk])
        p.op("gpsimd", lambda e, t=t, hb=hb: e.dma_start(out=xq1_h[t * 128:(t + 1) * 128, :], in_=hb[:]),
             reads=[hk], writes=[f"xq1H_{t}"], dma=True)

    def ln_body(xsrc, t0, half, nt, post):
        ln_stats4(half, nt)
        ln_chain4(half, nt)
        for i in range(nt):
            ln_apply4(half * 4 + i, half)
            post(t0 + i)

    _lnq = {"pending": None, "half": 0}

    def sched_group(xsrc, t0, nt, post):
        h = _lnq["half"]
        _lnq["half"] ^= 1
        ln_loads(xsrc, t0, h, nt)
        prev = _lnq["pending"]
        _lnq["pending"] = (xsrc, t0, h, nt, post)
        if prev is not None:
            ln_body(*prev)

    def sched_flush():
        prev = _lnq["pending"]
        _lnq["pending"] = None
        if prev is not None:
            ln_body(*prev)

    xnfm2 = {par: {(c, hl): BFA[:, par * 16384 + (hl * 8 + c) * 1024:
                                 par * 16384 + (hl * 8 + c + 1) * 1024]
                   for c in range(8) for hl in (0, 1)} for par in (0, 1)}
    p.op("gpsimd", lambda e: e.memset(vtt[0][:, 128:132], 0.0), writes=["vtt0c"])
    p.op("gpsimd", lambda e: e.memset(vtt[0][:, 128:129], 1.0), writes=["vtt0c"], fence=["vtt0c"])
    p.op("gpsimd", lambda e: e.memset(vtt[1][:, 128:132], 0.0), writes=["vtt1c"])
    PQ = (PS[0], PS[1])
    PK = (PS[2], PS[3])
    PV_ = (PS[4], PS[5])
    pkeys = {id(PS[i]): f"PSb{i}" for i in range(8)}

    def qkv_block(b):
        par = par_of_block[b]
        xnfm = xnfm2[par]
        for nm, pair, brk in (("q", PQ, "bq"), ("k", PK, "bk"), ("v", PV_, "bv")):
            for hh in range(2):
                p.op("tensor", lambda e, pair=pair, nm=nm, hh=hh, brk=brk: e.matmul(
                    pair[hh][:], brow[brk][0:1, :], ones512[0:1, :],
                    start=True, stop=False),
                    reads=[brk, "ones512"], writes=[pkeys[id(pair[hh])]])
            passes = [(0, 0), (1, 0), (0, 1)]
            for pi, (whl, xhl) in enumerate(passes):
                for c in range(8):
                    sp = (pi == 2 and c == 7)
                    for hh in range(2):
                        p.op("tensor", lambda e, pair=pair, nm=nm, whl=whl, xhl=xhl, c=c, sp=sp, hh=hh, xnfm=xnfm: e.matmul(
                            pair[hh][:], wqt[(nm, whl)][:, c * 128:(c + 1) * 128],
                            xnfm[(c, xhl)][:, hh * 512:(hh + 1) * 512], start=False, stop=sp),
                            reads=[f"wqt_{nm}{whl}", f"xnfm_{par}_{c}_{xhl}"],
                            writes=[pkeys[id(pair[hh])]])
        tag = f"qkv{b}"
        for nm, pair in (("k", PK), ("v", PV_), ("q", PQ)):
            if nm != "v":
                for hh in range(2):
                    p.op("scalar", lambda e, pair=pair, hh=hh: e.activation(
                        khi_b[:, hh * 512:(hh + 1) * 512], pair[hh][:], AF.Copy),
                        reads=[pkeys[id(pair[hh])]], writes=["khiB"])
                    p.op("vector", lambda e, pair=pair, hh=hh: e.scalar_tensor_tensor(
                        klo_b[:, hh * 512:(hh + 1) * 512], khi_b[:, hh * 512:(hh + 1) * 512],
                        -1.0, pair[hh][:], OP.mult, OP.add),
                        reads=["khiB", pkeys[id(pair[hh])]], writes=["kloB"])
            else:
                for hh in range(2):
                    p.op("scalar", lambda e, pair=pair, hh=hh: e.activation(
                        evA[:, hh * 512:(hh + 1) * 512], pair[hh][:], AF.Copy),
                        reads=[pkeys[id(pair[hh])]], writes=["evA"])
            if nm == "k":
                p.op("sync", lambda e, b=b: e.dma_start(out=kfm_h[0, :, b * CH:(b + 1) * CH], in_=khi_b[:]),
                     reads=["khiB"], writes=[f"kfmH0_{b}"], dma=True)
                p.op("sync", lambda e, b=b: e.dma_start(out=kfm_h[1, :, b * CH:(b + 1) * CH], in_=klo_b[:]),
                     reads=["kloB"], writes=[f"kfmH1_{b}"], dma=True)
            elif nm == "v":
                for i in range(8):
                    p.op("tensor", lambda e, i=i: e.transpose(
                        psT[:, 0:128], evA[:, i * 128:(i + 1) * 128], identf[:]),
                        reads=["evA", "identf"], writes=["PSb7a"])
                    p.op("scalar", lambda e: e.activation(vtt[0][:, 0:128], psT[:, 0:128], AF.Copy),
                         reads=["PSb7a"], writes=["vtt0"], fence=["vtt0c"])
                    p.op("vector", lambda e: e.scalar_tensor_tensor(
                        vtt[1][:, 0:128], vtt[0][:, 0:128], -1.0, psT[:, 0:128], OP.mult, OP.add),
                        reads=["PSb7a", "vtt0"], writes=["vtt1"], fence=["vtt1c"])
                    for hl in (0, 1):
                        p.op("sync", lambda e, hl=hl, b=b, i=i: e.dma_start(
                            out=vtm_h[hl, (b * 8 + i) * 128:(b * 8 + i + 1) * 128, :], in_=vtt[hl][:]),
                            reads=[f"vtt{hl}", f"vtt{hl}c"], writes=[f"vtmH_{hl}_{b}_{i}"], dma=True)
            else:
                if b >= 7:
                    p.op("sync", lambda e, b=b: e.dma_start(
                        out=qall_h[0][:, (b - 7) * CH:(b - 6) * CH], in_=khi_b[:]),
                        reads=["khiB"], writes=[f"qallH0_{b}"], dma=True)
                    p.op("sync", lambda e, b=b: e.dma_start(
                        out=qall_h[1][:, (b - 7) * CH:(b - 6) * CH], in_=klo_b[:]),
                        reads=["kloB"], writes=[f"qallH1_{b}"], dma=True)

    uq = (mk_quant_units(w1_d, DH, D, w1q_h, 0, "w1")
          + mk_quant_units(w2_d, D, DH, w2q_h, 1, "w2"))
    kv_order = kv_consume_order
    own_groups = [(0, 4), (4, 4), (8, 4), (12, 4), (16, 2)]
    per = (len(uq) + 8) // 9
    for idx, b in enumerate(kv_order):
        sched_group(x_kv, b * 8, 4, kv_post)
        sched_group(x_kv, b * 8 + 4, 4, kv_post)
        if idx >= 4:
            og = own_groups[idx - 4]
            sched_group(x_own, og[0], og[1], own_post)
        if idx >= 1:
            qkv_block(kv_order[idx - 1])
        for u in uq[idx * per:(idx + 1) * per]:
            u()
    sched_flush()
    qkv_block(kv_order[-1])
    for u in uq[9 * per:]:
        u()

    qall_done = [f"qallH{hl}_{b}" for hl in (0, 1) for b in (7, 8)]
    kv_keys = [f"kfmH{hl}_{b}" for hl in (0, 1) for b in range(NBLK)]
    vt_keys = [f"vtmH_{hl}_{b}_{i}" for hl in (0, 1) for b in range(NBLK) for i in range(8)]

    w1t_keys = []
    for c in range(8):
        kk = f"W1T_{c}"
        w1t_keys.append(kk)
        p.op("sync", lambda e, c=c: e.dma_start(
            out=BFA[:, c * 4096:(c + 1) * 4096], in_=w1q_h[:, c * 128:(c + 1) * 128],
            transpose=True),
            reads=["wqH_w1"], writes=[kk], dma=True,
            fence=[f"xnfm_{par}_{cc}_{hl}" for par in (0, 1) for cc in range(8) for hl in (0, 1)])
    g_all = FA
    nq_sub = BFB[:, 0:8192]
    w2qt = BFB[:, 8192:16384]

    for cc in range(2):
        for sub in range(4):
            g0 = sub * 256
            stag = f"f{cc}_{sub}"
            r0 = cc * GRID + g0
            for c in range(8):
                p.op("sync", lambda e, c=c, r0=r0: e.dma_start(
                    out=xqf[:, c * 272:(c + 1) * 272],
                    in_=xq1_h[r0:r0 + 272, c * 128:(c + 1) * 128], transpose=True),
                    reads=[f"xq1H_{r0 // 128 + i}" for i in range(3)],
                    writes=["xqf"], dma=True)
            for i in range(3):
                tl = cc * 9 + 2 * sub + i
                p.op("vector", lambda e, tl=tl: e.tensor_tensor(
                    diagt[:], identf[:], s1c[:, tl:tl + 1].broadcast_to([128, 128]), OP.mult),
                    reads=["s1c", "identf"], writes=["diagt"])
                p.op("tensor", lambda e: e.matmul(PS[1][:, 0:128], ones_f[:], diagt[:],
                                                  start=True, stop=True),
                     reads=["diagt", "ones_f"], writes=["PSb1"])
                p.op("vector", lambda e, i=i: e.tensor_copy(s1bc[:, i * 128:(i + 1) * 128],
                                                            PS[1][:, 0:128]),
                     reads=["PSb1"], writes=["s1bc"])
            for par in range(2):
                p.op("gpsimd", lambda e, par=par: e.memset(M1[par][:], 0.0), writes=[f"M1_{par}"],
                     fence=["FA2"])
                p.op("gpsimd", lambda e, par=par: e.memset(M1n[par][:], 0.0), writes=[f"M1n_{par}"])
            def mm_minmax(hid):
                par = hid % 2
                p.op("vector", lambda e, hid=hid, par=par: e.tensor_tensor(
                    M1[par][:], M1[par][:], g_all[:, hid * 256:(hid + 1) * 256], OP.max),
                    reads=[f"gq_{hid // 4}", f"M1_{par}"], writes=[f"M1_{par}"])
                p.op("vector", lambda e, hid=hid, par=par: e.tensor_tensor(
                    M1n[par][:], M1n[par][:], g_all[:, hid * 256:(hid + 1) * 256], OP.min),
                    reads=[f"gq_{hid // 4}", f"M1n_{par}"], writes=[f"M1n_{par}"])

            for hid in range(32):
                par = hid % 2
                psh = PS[0] if par == 0 else PS[6]
                pshk = "PSb0" if par == 0 else "PSb6"
                for c in range(8):
                    p.op("tensor", lambda e, c=c, hid=hid, psh=psh: e.matmul(
                        psh[:, 0:258], BFA[:, c * 4096 + hid * 128:c * 4096 + (hid + 1) * 128],
                        xqf[:, c * 272:c * 272 + 258], start=(c == 0), stop=(c == 7)),
                        reads=["xqf", f"W1T_{c}"], writes=[pshk])
                p.op("vector", lambda e, par=par, psh=psh: e.tensor_tensor(
                    hts[par][:], psh[:, 0:258], s1bc[:, 0:258], OP.mult),
                    reads=[pshk, "s1bc"], writes=[f"hts{par}"])
                p.op("vector", lambda e, hid=hid, par=par: e.tensor_scalar(
                    cv1[par][:], hts[par][:, 1:257], cw_sb[:, 32 + hid:33 + hid], None, OP.mult),
                    reads=[f"hts{par}", "cw_sb"], writes=[f"cv1_{par}"])
                p.op("vector", lambda e, hid=hid, par=par: e.tensor_scalar(
                    cv2[par][:], hts[par][:, 0:256], cw_sb[:, hid:hid + 1], None, OP.mult),
                    reads=[f"hts{par}", "cw_sb"], writes=[f"cv2_{par}"])
                p.op("vector", lambda e, par=par: e.tensor_tensor(cv2[par][:], cv2[par][:], cv1[par][:], OP.add),
                    reads=[f"cv2_{par}", f"cv1_{par}"], writes=[f"cv2_{par}"])
                p.op("vector", lambda e, hid=hid, par=par: e.scalar_tensor_tensor(
                    cv1[par][:], hts[par][:, 2:258], cw_sb[:, 64 + hid:65 + hid], cv2[par][:],
                    OP.mult, OP.add),
                    reads=[f"hts{par}", f"cv2_{par}", "cw_sb"], writes=[f"cv1_{par}"])
                lnf = ([f"lnx_{hid // 4}"] if (cc, sub) == (0, 0) and hid % 4 == 0 else [])
                p.op("scalar", lambda e, hid=hid, par=par: e.activation(
                    g_all[:, hid * 256:(hid + 1) * 256], cv1[par][:], AF.Silu,
                    bias=cb_sb[:, hid:hid + 1]),
                    reads=[f"cv1_{par}", "cb_sb"], writes=[f"gq_{hid // 4}"], fence=lnf)
                if hid > 0:
                    mm_minmax(hid - 1)
            mm_minmax(31)
            p.op("vector", lambda e: e.tensor_tensor(M1[0][:], M1[0][:], M1[1][:], OP.max),
                 reads=["M1_0", "M1_1"], writes=["M1_0"])
            p.op("vector", lambda e: e.tensor_tensor(M1n[0][:], M1n[0][:], M1n[1][:], OP.min),
                 reads=["M1n_0", "M1n_1"], writes=["M1n_0"])
            for i in range(2):
                p.op("tensor", lambda e, i=i: e.transpose(psT[:, 0:128], M1[0][:, i * 128:(i + 1) * 128],
                                                          identf[:]),
                     reads=["M1_0", "identf"], writes=["PSb7a"])
                p.op("vector", lambda e: e.tensor_reduce(colsc[:, 4:5], psT[:, 0:128], AX, OP.max),
                     reads=["PSb7a"], writes=["colsc45"])
                p.op("tensor", lambda e, i=i: e.transpose(psT[:, 0:128], M1n[0][:, i * 128:(i + 1) * 128],
                                                          identf[:]),
                     reads=["M1n_0", "identf"], writes=["PSb7a"], fence=["colsc45"])
                p.op("vector", lambda e: e.tensor_reduce(colsc[:, 3:4], psT[:, 0:128], AX, OP.min,
                                                         negate=True),
                     reads=["PSb7a"], writes=["colsc34"])
                p.op("vector", lambda e: e.tensor_tensor(colsc[:, 4:5], colsc[:, 4:5], colsc[:, 3:4],
                                                         OP.max),
                     reads=["colsc34", "colsc45"], writes=["colsc45"])
                p.op("vector", lambda e: e.tensor_scalar(colsc[:, 5:6], colsc[:, 4:5], 1e-5, None, OP.max),
                     reads=["colsc45"], writes=["colsc56"])
                p.op("vector", lambda e: e.reciprocal(colsc[:, 6:7], colsc[:, 5:6]),
                     reads=["colsc56"], writes=["colsc67"])
                p.op("vector", lambda e, i=i: e.tensor_scalar(colsc[:, 7 + i:8 + i], colsc[:, 6:7],
                                                              127.0, None, OP.mult),
                     reads=["colsc67"], writes=["colsc78"])
                tl = cc * 8 + 2 * sub + i
                p.op("vector", lambda e, tl=tl: e.scalar_tensor_tensor(
                    s2c[:, tl:tl + 1], colsc[:, 5:6], 1.0 / 127.0, wsc[:, 3:4], OP.mult, OP.mult),
                    reads=["colsc56", "wsc3"], writes=["s2c"])
                p.op("vector", lambda e, i=i: e.tensor_tensor(
                    diagt[:], identf[:], colsc[:, 7 + i:8 + i].broadcast_to([128, 128]), OP.mult),
                    reads=["colsc78", "identf"], writes=["diagt"])
                p.op("tensor", lambda e: e.matmul(PS[1][:, 0:128], ones_f[:], diagt[:],
                                                  start=True, stop=True),
                     reads=["diagt", "ones_f"], writes=["PSb1"])
                p.op("vector", lambda e, i=i: e.tensor_copy(ascbc[:, i * 128:(i + 1) * 128],
                                                            PS[1][:, 0:128]),
                     reads=["PSb1"], writes=["s1bc"])
            for gidx in range(8):
                gk = f"gq_{gidx}"
                gsl = g_all[:, gidx * 1024:(gidx + 1) * 1024]
                a3 = ascbc[:, 0:256].rearrange("p t -> p () t").broadcast_to([128, 4, 256])
                p.op("vector", lambda e, gsl=gsl: e.tensor_scalar(
                    sm2[:], gsl, 0.0, 0.5, OP.is_ge, OP.subtract),
                    reads=[gk], writes=["sm2"])
                p.op("vector", lambda e, gsl=gsl, a3=a3: e.tensor_tensor(
                    gsl.rearrange("p (g t) -> p g t", g=4), gsl.rearrange("p (g t) -> p g t", g=4),
                    a3, OP.mult),
                    reads=["s1bc", gk], writes=[gk])
                p.op("vector", lambda e, gsl=gsl: e.tensor_tensor(smi[:], gsl, sm2[:], OP.add),
                     reads=[gk, "sm2"], writes=["smi"])
                p.op("vector", lambda e, gsl=gsl: e.tensor_tensor(sm4[:], gsl, smi[:], OP.subtract),
                    reads=["smi", gk], writes=["sm4"])
                p.op("vector", lambda e: e.tensor_tensor(
                    sm2[:], sm4[:], ch05[:], OP.is_gt),
                    reads=["sm4", "ch05"], writes=["sm2"])
                p.op("vector", lambda e: e.tensor_tensor(sm2[:], sm2[:], smi[:], OP.add),
                    reads=["sm2", "smi"], writes=["sm2"])
                p.op("vector", lambda e, gidx=gidx: e.scalar_tensor_tensor(
                    nq_sub[:, gidx * 1024:(gidx + 1) * 1024], sm4[:], -0.5, sm2[:],
                    OP.is_lt, OP.subtract),
                    reads=["sm4", "sm2"], writes=["nq"],
                    fence=[f"wqt_{n}{h}" for n in 'qkv' for h in (0, 1)] + ["khiB", "kloB"])
            for qtr in range(4):
                for hh in range(8):
                    hid = qtr * 8 + hh
                    p.op("sync", lambda e, hid=hid, hh=hh: e.dma_start(
                        out=w2qt[:, hh * 1024:(hh + 1) * 1024],
                        in_=w2q_h[:, hid * 128:(hid + 1) * 128], transpose=True),
                        reads=["wqH_w2"], writes=["w2qt"], dma=True,
                        fence=["bh1B", "bh2B"])
                for tt in range(2):
                    pair = (PS[2], PS[3]) if tt == 0 else (PS[4], PS[5])
                    for hh in range(8):
                        for half in range(2):
                            p.op("tensor", lambda e, pair=pair, tt=tt, hh=hh, half=half, qtr=qtr: e.matmul(
                                pair[half][:],
                                nq_sub[:, (qtr * 8 + hh) * 256 + tt * 128:(qtr * 8 + hh) * 256 + tt * 128 + 128],
                                w2qt[:, hh * 1024 + half * 512:hh * 1024 + (half + 1) * 512],
                                start=(qtr == 0 and hh == 0), stop=(qtr == 3 and hh == 7)),
                                reads=["nq", "w2qt"], writes=[pkeys[id(pair[half])]])
            for tt in range(2):
                pair = (PS[2], PS[3]) if tt == 0 else (PS[4], PS[5])
                own0 = cc * CH + sub * 256 + tt * 128
                grid_r = cc * GRID + sub * 256 + tt * 128 + 1
                p.op("sync", lambda e, grid_r=grid_r: e.dma_start(
                    out=evA[:], in_=x_own[grid_r:grid_r + 128, :]),
                    writes=["evA"], dma=True)
                tl = cc * 8 + sub * 2 + tt
                for half in range(2):
                    p.op("vector", lambda e, pair=pair, half=half, tl=tl: e.scalar_tensor_tensor(
                        sm4[:, half * 512:(half + 1) * 512], pair[half][:], s2c[:, tl:tl + 1],
                        evA[:, half * 512:(half + 1) * 512], OP.mult, OP.add),
                        reads=[pkeys[id(pair[half])], "s2c", "evA"], writes=["sm4"])
                p.op("gpsimd", lambda e, own0=own0: e.dma_start(
                    out=part_h[own0:own0 + 128, :], in_=sm4[:]),
                    reads=["sm4"], writes=[f"partH_{own0}"], dma=True)

    part_keys = [f"partH_{o}" for o in range(0, 2 * CH, 128)]

    build_tri_masks()
    qs_hi = BFB[:, 0:1024]
    qs_lo = BFB[:, 1024:2048]
    ks_hi = BFB[:, 2048:3072]
    ks_lo = BFB[:, 3072:4096]
    vt_hi = BFB[:, 4096:5152]
    vt_lo = BFB[:, 5152:6208]
    pT_hi = BFA[:, 0:8192]
    pT_lo = BFA[:, 8192:16384]
    O_loc = FA[:, 0:1056]
    O_acc = {0: FA[:, 1056:2112], 1: FA[:, 2112:3168]}
    pf = evA

    import bass_rust as _br2
    slot_rq = []
    for s in range(NBLK):
        hq = nc.gpsimd.alloc_register(f"rq{s}")
        slot_rq.append((hq, _br2.RuntimeValue(_br2.RegisterHandles([hq]))))

    def load_slot_regs(e):
        last = None
        for s in range(NBLK):
            insts = nc.regs_load(_br2.RegisterHandles([slot_rq[s][0]]), sel_sb[0:1, 18 + s:19 + s])
            if insts:
                last = insts[-1]
        return last
    p.op("gpsimd", load_slot_regs, reads=["sel_sb"], writes=["slotregs"])

    p.op("gpsimd", lambda e: e.memset(mAB[:, 0:16], NEG), writes=["mAB"])
    p.op("gpsimd", lambda e: e.memset(O_loc[:], 0.0), writes=["FA"],
         fence=[f"gq_{i}" for i in range(8)])
    p.op("gpsimd", lambda e: e.memset(O_acc[0][:], 0.0), writes=["FA"])
    p.op("gpsimd", lambda e: e.memset(O_acc[1][:], 0.0), writes=["FA"])

    bfb_fence = ["nq", "w2qt"]
    bfa_fence = list(w1t_keys)

    def slot_loads(s, need_v):
        rq = slot_rq[s][1]
        p.op("gpsimd", lambda e, rq=rq: e.dma_start(out=qs_hi, in_=qall_h[0][:, DS(rq, CH)]),
             reads=qall_done + ["slotregs"], writes=["qs_hi"], dma=True, fence=bfb_fence)
        p.op("gpsimd", lambda e, rq=rq: e.dma_start(out=qs_lo, in_=qall_h[1][:, DS(rq, CH)]),
             reads=qall_done + ["slotregs"], writes=["qs_lo"], dma=True, fence=bfb_fence)
        p.op("sync", lambda e, s=s: e.dma_start(out=ks_hi, in_=kfm_h[0, :, s * CH:(s + 1) * CH]),
             reads=kv_keys, writes=["ks_hi"], dma=True, fence=bfb_fence)
        p.op("sync", lambda e, s=s: e.dma_start(out=ks_lo, in_=kfm_h[1, :, s * CH:(s + 1) * CH]),
             reads=kv_keys, writes=["ks_lo"], dma=True, fence=bfb_fence)
        if need_v:
            for hl, dst, dk in ((0, vt_hi, "vt_hi"), (1, vt_lo, "vt_lo")):
                p.op("sync", lambda e, hl=hl, dst=dst, s=s: e.dma_start(
                    out=dst.rearrange("p (i c) -> p i c", i=8),
                    in_=vtm_h[hl, :, :].rearrange("(i p) c -> p i c", p=128)[:, s * 8:(s + 1) * 8]),
                    reads=vt_keys, writes=[dk], dma=True, fence=bfb_fence)

    for s in range(NBLK):
        slot_loads(s, need_v=False)
        for qt in range(8):
            pair = (PS[0], PS[1]) if qt % 2 == 0 else (PS[2], PS[3])
            kvar = ((qs_hi, "qs_hi", ks_hi, "ks_hi"),
                    (qs_lo, "qs_lo", ks_hi, "ks_hi"),
                    (qs_hi, "qs_hi", ks_lo, "ks_lo"))
            for i, (qb, qk, kb, kk) in enumerate(kvar):
                for hh in range(2):
                    p.op("tensor", lambda e, pair=pair, qb=qb, kb=kb, qt=qt, i=i, hh=hh: e.matmul(
                        pair[hh][:], qb[:, qt * 128:(qt + 1) * 128],
                        kb[:, hh * 512:(hh + 1) * 512], start=(i == 0), stop=(i == 2)),
                        reads=[qk, kk], writes=[pkeys[id(pair[hh])]])
            for hh in range(2):
                if s >= 7:
                    p.op("vector", lambda e, pair=pair, qt=qt, hh=hh: e.tensor_tensor(
                        pair[hh][:], pair[hh][:],
                        tri_qk[:, (8 - qt) * 128 + hh * 512:(8 - qt) * 128 + (hh + 1) * 512], OP.add),
                        reads=[pkeys[id(pair[hh])], "tri_qk"], writes=[pkeys[id(pair[hh])]])
                p.op("vector", lambda e, pair=pair, hh=hh: e.tensor_reduce(
                    colsc[:, 9 + hh:10 + hh], pair[hh][:], AX, OP.max),
                    reads=[pkeys[id(pair[hh])]], writes=["colsc9A"])
            p.op("vector", lambda e, qt=qt: e.tensor_tensor(
                mloc[:, qt:qt + 1], colsc[:, 9:10], colsc[:, 10:11], OP.max),
                reads=["colsc9A"], writes=["mloc"])
        for side in range(2):
            p.op("vector", lambda e, s=s, side=side: e.scalar_tensor_tensor(
                nmst[:], mloc[:, 0:8], selw[:, side * 9 + s:side * 9 + s + 1],
                selw[:, 18 + side * 9 + s:18 + side * 9 + s + 1].broadcast_to([128, 8]),
                OP.mult, OP.add),
                reads=["mloc", "selw"], writes=["nmst"])
            p.op("vector", lambda e, side=side: e.tensor_tensor(
                mAB[:, side * 8:(side + 1) * 8], mAB[:, side * 8:(side + 1) * 8], nmst[:], OP.max),
                reads=["nmst", "mAB"], writes=["mAB"])

    p.op("vector", lambda e: e.tensor_scalar(mAB[:, 16:32], mAB[:, 0:16], -1.0, None, OP.mult),
         reads=["mAB"], writes=["mAB"])

    for s in range(NBLK):
        slot_loads(s, need_v=True)
        p.op("vector", lambda e, s=s: e.tensor_scalar(
            nmst[:], mAB[:, 16:24], selw[:, s:s + 1], None, OP.mult),
            reads=["mAB", "selw"], writes=["nmst"])
        p.op("vector", lambda e, s=s: e.scalar_tensor_tensor(
            nmst[:], mAB[:, 24:32], selw[:, 9 + s:10 + s], nmst[:], OP.mult, OP.add),
            reads=["mAB", "selw", "nmst"], writes=["nmst"])
        for qt in range(8):
            p.op("vector", lambda e, qt=qt: e.tensor_tensor(
                diagS[:, qt * 128:(qt + 1) * 128], identb[:],
                nmst[:, qt:qt + 1].broadcast_to([128, 128]), OP.mult),
                reads=["nmst", "identb"], writes=["xqf"])
        for kc in range(8):
            pair = (PS[0], PS[1]) if kc % 2 == 0 else (PS[2], PS[3])
            kvar = ((ks_hi, "ks_hi", qs_hi, "qs_hi"),
                    (ks_lo, "ks_lo", qs_hi, "qs_hi"),
                    (ks_hi, "ks_hi", qs_lo, "qs_lo"))
            for i, (kb, kk, qb, qk) in enumerate(kvar):
                for hh in range(2):
                    p.op("tensor", lambda e, pair=pair, kb=kb, qb=qb, kc=kc, i=i, hh=hh: e.matmul(
                        pair[hh][:], kb[:, kc * 128:(kc + 1) * 128],
                        qb[:, hh * 512:(hh + 1) * 512], start=(i == 0), stop=False),
                        reads=[kk, qk], writes=[pkeys[id(pair[hh])]])
            for hh in range(2):
                p.op("tensor", lambda e, pair=pair, hh=hh: e.matmul(
                    pair[hh][:], ones_b[:], diagS[:, hh * 512:(hh + 1) * 512],
                    start=False, stop=True),
                    reads=["xqf", "ones_b"], writes=[pkeys[id(pair[hh])]])
                if s >= 7:
                    p.op("vector", lambda e, pair=pair, kc=kc, hh=hh: e.tensor_tensor(
                        pair[hh][:], pair[hh][:],
                        tri_T[:, (8 - kc) * 128 + hh * 512:(8 - kc) * 128 + (hh + 1) * 512], OP.add),
                        reads=[pkeys[id(pair[hh])], "tri_T"], writes=[pkeys[id(pair[hh])]])
                p.op("scalar", lambda e, pair=pair, hh=hh: e.activation(
                    pf[:, hh * 512:(hh + 1) * 512], pair[hh][:], AF.Exp, scale=SCALE),
                    reads=[pkeys[id(pair[hh])]], writes=["evA"])
            p.op("vector", lambda e, kc=kc: e.tensor_copy(pT_hi[:, kc * 1024:(kc + 1) * 1024], pf[:]),
                 reads=["evA"], writes=["pT_hi"], fence=bfa_fence)
            p.op("vector", lambda e, kc=kc: e.scalar_tensor_tensor(
                pT_lo[:, kc * 1024:(kc + 1) * 1024], pT_hi[:, kc * 1024:(kc + 1) * 1024], -1.0,
                pf[:], OP.mult, OP.add),
                reads=["pT_hi", "evA"], writes=["pT_lo"], fence=bfa_fence)
        for qt in range(8):
            pv = PS[4] if qt % 2 == 0 else PS[5]
            pvk = pkeys[id(pv)]
            nmm = 0
            for kc in range(8):
                for (pb, pk, vb, vk) in ((pT_hi, "pT_hi", vt_hi, "vt_hi"),
                                         (pT_hi, "pT_hi", vt_lo, "vt_lo"),
                                         (pT_lo, "pT_lo", vt_hi, "vt_hi")):
                    p.op("tensor", lambda e, pv=pv, pb=pb, vb=vb, kc=kc, qt=qt,
                         st=(nmm == 0), sp=(nmm == 23): e.matmul(
                        pv[:, 0:132], pb[:, kc * 1024 + qt * 128:kc * 1024 + (qt + 1) * 128],
                        vb[:, kc * 132:(kc + 1) * 132], start=st, stop=sp),
                        reads=[pk, vk], writes=[pvk])
                    nmm += 1
            p.op("vector", lambda e, pv=pv, qt=qt: e.tensor_tensor(
                O_loc[:, qt * 132:(qt + 1) * 132], pv[:, 0:132],
                O_loc[:, qt * 132:(qt + 1) * 132], OP.add),
                reads=[pvk, "FA"], writes=["FA"])
        for side in range(2):
            p.op("vector", lambda e, s=s, side=side: e.scalar_tensor_tensor(
                O_acc[side][:], O_loc[:], selw[:, side * 9 + s:side * 9 + s + 1],
                O_acc[side][:], OP.mult, OP.add),
                reads=["FA", "selw"], writes=["FA"])
        p.op("gpsimd", lambda e: e.memset(O_loc[:, 0:1056], 0.0), reads=["FA"], writes=["FA"])

    for g in range(16):
        Osl = O_acc[g // 8][:, (g % 8) * 132:(g % 8) * 132 + 132]
        p.op("vector", lambda e, Osl=Osl: e.reciprocal(colsc[:, 12:13], Osl[:, 128:129]),
             reads=["FA"], writes=["colsc12"])
        p.op("vector", lambda e, Osl=Osl: e.tensor_scalar(
            sm1[:, 0:128], Osl[:, 0:128], colsc[:, 12:13], None, OP.mult),
            reads=["FA", "colsc12"], writes=["sm1"])
        p.op("tensor", lambda e: e.transpose(psT[:, 0:128], sm1[:, 0:128], identf[:]),
             reads=["sm1", "identf"], writes=["PSb7a"])
        p.op("scalar", lambda e: e.activation(bh1[:, 0:128], psT[:, 0:128], AF.Copy),
             reads=["PSb7a"], writes=["bh1"])
        p.op("vector", lambda e: e.scalar_tensor_tensor(
            bh2[:, 0:128], bh1[:, 0:128], -1.0, psT[:, 0:128], OP.mult, OP.add),
            reads=["bh1", "PSb7a"], writes=["bh2"])
        mmspec = ((bh1, "bh1", wot[0], "wot0"),
                  (bh1, "bh1", wot[1], "wot1"),
                  (bh2, "bh2", wot[0], "wot0"))
        for i, (ob, obk, wt, wtk) in enumerate(mmspec):
            for hh in range(2):
                p.op("tensor", lambda e, ob=ob, wt=wt, i=i, hh=hh: e.matmul(
                    PS[hh][:], ob[:, 0:128], wt[:, hh * 512:(hh + 1) * 512],
                    start=(i == 0), stop=False),
                    reads=[obk, wtk], writes=[pkeys[id(PS[hh])]])
        for hh in range(2):
            p.op("tensor", lambda e, hh=hh: e.matmul(
                PS[hh][:], ones_f[0:1, :], bo_row[:, hh * 512:(hh + 1) * 512],
                start=False, stop=True),
                reads=["bo_row", "ones_f"], writes=[pkeys[id(PS[hh])]])
        p.op("sync", lambda e, g=g: e.dma_start(out=evA[:], in_=part_h[g * 128:(g + 1) * 128, :]),
             reads=part_keys, writes=["evA"], dma=True)
        for hh in range(2):
            p.op("vector", lambda e, hh=hh: e.tensor_tensor(
                sm4[:, hh * 512:(hh + 1) * 512], PS[hh][:], evA[:, hh * 512:(hh + 1) * 512], OP.add),
                reads=[pkeys[id(PS[hh])], "evA"], writes=["sm4"])
        p.op("gpsimd", lambda e, g=g: e.dma_start(out=out_d[g * 128:(g + 1) * 128, :], in_=sm4[:]),
             reads=["sm4"], writes=[f"outH_{g}"], dma=True)

    with nc.Block() as block:
        p.emit(block)
    return nc


_NC_CACHE = None


def _get_nc():
    global _NC_CACHE
    if _NC_CACHE is None:
        _NC_CACHE = build_program()
    return _NC_CACHE


def _make_inmaps(x, wq, wk, wv, wo, bq, bk, bv, bo, w1, w2, cw, cb):
    in_maps = []
    for j in range(8):
        slots = [(0, bb) for bb in range(j)] + [(1, bb) for bb in range(7 - j)]
        slots += [(0, j), (1, 7 - j)]
        x_kv = np.concatenate([x[bat, bb * CH:(bb + 1) * CH] for bat, bb in slots], axis=0)
        x_kv = np.ascontiguousarray(x_kv)
        x_own = np.zeros((OWNP, D), np.float32)
        for cc, (bb, c0) in enumerate(((0, CH * j), (1, CH * (7 - j)))):
            lo, hi = c0 - 1, c0 + CH + 1
            slo, shi = max(lo, 0), min(hi, S)
            x_own[cc * GRID + (slo - lo): cc * GRID + (shi - lo)] = x[bb, slo:shi]
        sel = np.zeros((1, 64), np.int32)
        selw = np.zeros((128, 36), np.float32)
        selw[:, 18:36] = NEG
        for s, (bat, bb) in enumerate(slots):
            sel[0, 18 + s] = 0 if bat == 0 else CH
            selw[:, (0 if bat == 0 else 9) + s] = 1.0
            selw[:, 18 + (0 if bat == 0 else 9) + s] = 0.0
        in_maps.append(dict(
            x_kv=x_kv, x_own=x_own, wq=wq, wk=wk, wv=wv, wo=wo, bq=bq, bk=bk, bv=bv,
            bo=bo, w1=w1, w2=w2, cw=cw, cb=cb, seltab=sel, selw_in=selw))
    return in_maps


def kernel(L0_attn_x, L0_attn_Wq, L0_attn_bq, L0_attn_Wk, L0_attn_bk, L0_attn_Wv,
           L0_attn_bv, L0_attn_Wo, L0_attn_bo, L0_ffn_w1, L0_ffn_conv_w, L0_ffn_conv_b,
           L0_ffn_w2):
    x = np.ascontiguousarray(np.asarray(L0_attn_x, dtype=np.float32))
    wq = np.ascontiguousarray(np.asarray(L0_attn_Wq, dtype=np.float32))
    wk = np.ascontiguousarray(np.asarray(L0_attn_Wk, dtype=np.float32))
    wv = np.ascontiguousarray(np.asarray(L0_attn_Wv, dtype=np.float32))
    wo = np.ascontiguousarray(np.asarray(L0_attn_Wo, dtype=np.float32))
    bq = np.asarray(L0_attn_bq, dtype=np.float32).reshape(1, H)
    bk = np.asarray(L0_attn_bk, dtype=np.float32).reshape(1, H)
    bv = np.asarray(L0_attn_bv, dtype=np.float32).reshape(1, H)
    bo = np.asarray(L0_attn_bo, dtype=np.float32).reshape(1, D)
    w1 = np.ascontiguousarray(np.asarray(L0_ffn_w1, dtype=np.float32))
    w2 = np.ascontiguousarray(np.asarray(L0_ffn_w2, dtype=np.float32))
    cw3 = np.asarray(L0_ffn_conv_w, dtype=np.float32)[:, 0, :]
    cw = np.ascontiguousarray(
        cw3.reshape(32, 128, 3).transpose(1, 2, 0).reshape(128, 96))
    cb = np.ascontiguousarray(
        np.asarray(L0_ffn_conv_b, dtype=np.float32).reshape(32, 128).T)

    in_maps = _make_inmaps(x, wq, wk, wv, wo, bq, bk, bv, bo, w1, w2, cw, cb)
    nc = _get_nc()
    res = run_bass_kernel_spmd(nc, in_maps, core_ids=list(range(8)))
    out = np.empty((B, S, D), np.float32)
    for j in range(8):
        o = np.asarray(res.results[j]["out_own"])
        out[0, CH * j:CH * (j + 1)] = o[0:CH]
        out[1, CH * (7 - j):CH * (8 - j)] = o[CH:2 * CH]
    return out


# version 16
# speedup vs baseline: 1.0942x; 1.0930x over previous
import numpy as np
import ml_dtypes
import concourse.bass as bass
import concourse.mybir as mybir
from concourse.bass_utils import run_bass_kernel_spmd

F32 = mybir.dt.float32
BF16 = mybir.dt.bfloat16
I32 = mybir.dt.int32
AX = mybir.AxisListType.X
OP = mybir.AluOpType
AF = mybir.ActivationFunctionType
ET = mybir.EngineType
DS = bass.ds

B, S, D, H, DH = 2, 8192, 1024, 128, 4096
CH = 1024
KV = 9216
NBLK = KV // CH
GRID = 1152
OWNP = 2 * GRID
SCALE = float(np.float32(128.0 ** -0.5))
NEG = -1.0e30

ENGINES = ("tensor", "vector", "scalar", "gpsimd", "sync")


class Prog:

    def __init__(self, nc):
        self.nc = nc
        self.ops = []
        self.last_writer = {}
        self.readers = {}

    def op(self, engine, fn, reads=(), writes=(), dma=False, fence=()):
        i = len(self.ops)
        deps = set()
        for b in reads:
            if b in self.last_writer:
                deps.add(self.last_writer[b])
        for b in list(writes) + list(fence):
            if b in self.last_writer:
                deps.add(self.last_writer[b])
            for r in self.readers.get(b, ()):
                deps.add(r)
        deps.discard(i)
        self.ops.append((engine, fn, deps, dma))
        for b in writes:
            self.last_writer[b] = i
            self.readers[b] = []
        for b in reads:
            self.readers.setdefault(b, []).append(i)
        return i

    def emit(self, block):
        nc = self.nc
        n = len(self.ops)
        needed = [False] * n
        for (_, _, deps, _) in self.ops:
            for d in deps:
                needed[d] = True
        NROT = 8
        sems = {}
        for e in ENGINES:
            sems[(e, False)] = nc.alloc_semaphore(f"s_{e}_c")
            for r in range(NROT):
                sems[(e, True, r)] = nc.alloc_semaphore(f"s_{e}_d{r}")
        count = [0] * n
        slot = [None] * n
        cur_c = {e: 0 for e in ENGINES}
        cur_d = {(e, r): 0 for e in ENGINES for r in range(NROT)}
        nseen_d = {e: 0 for e in ENGINES}
        for i, (eng, fn, deps, isd) in enumerate(self.ops):
            if isd:
                r = nseen_d[eng] % NROT
                nseen_d[eng] += 1
                slot[i] = r
                cur_d[(eng, r)] += 16
                count[i] = cur_d[(eng, r)]
            elif needed[i]:
                cur_c[eng] += 1
                count[i] = cur_c[eng]
            else:
                count[i] = -1
        final_c = dict(cur_c)
        final_d = dict(cur_d)
        streams = {e: [] for e in ENGINES}
        waited = {e: {} for e in ENGINES}

        def add_wait(waits, eng, key, c):
            if waited[eng].get(key, 0) >= c:
                return
            waits.append((key, c))
            waited[eng][key] = c

        for i, (eng, fn, deps, isd) in enumerate(self.ops):
            waits = []
            if isd:
                key = (eng, True, slot[i])
                prior = count[i] - 16
                if prior > 0:
                    add_wait(waits, eng, key, prior)
            for d in sorted(deps):
                peng, pisd = self.ops[d][0], self.ops[d][3]
                c = count[d]
                if c < 0:
                    c = None
                    for k in range(d + 1, n):
                        if (self.ops[k][0], self.ops[k][3]) == (peng, pisd) and not pisd                                 and count[k] > 0:
                            c = count[k]
                            break
                    if c is None:
                        continue
                key = (peng, True, slot[d]) if pisd else (peng, False)
                add_wait(waits, eng, key, c)
            streams[eng].append((waits, fn, count[i] if count[i] > 0 else 0, isd,
                                 slot[i]))

        def make_body(e_name):
            def body(engine_obj):
                for waits, fn, cnt, isd, sl in streams[e_name]:
                    for key, c in waits:
                        engine_obj.wait_ge(sems[key], c)
                    inst = fn(engine_obj)
                    if inst is not None:
                        if isd:
                            inst.then_inc(sems[(e_name, True, sl)], 16)
                        elif cnt > 0:
                            inst.then_inc(sems[(e_name, False)], 1)
                for r in range(NROT):
                    if final_d[(e_name, r)] > 0:
                        engine_obj.wait_ge(sems[(e_name, True, r)], final_d[(e_name, r)])
            return body

        block.tensor(make_body("tensor"))
        block.vector(make_body("vector"))
        block.scalar(make_body("scalar"))
        block.gpsimd(make_body("gpsimd"))
        block.sync(make_body("sync"))


def build_program():
    nc = bass.Bass("TRN2", target_bir_lowering=False)
    p = Prog(nc)
    import bass_rust as _br

    def _vload(ap):
        v = nc.values_load(ap, engines=[ET.Pool], skip_runtime_bounds_check=True)
        return v

    def _vfree(v):
        nc.gpsimd.free_register(list(v.val)[0])

    x_kv = nc.dram_tensor("x_kv", [KV, D], F32, kind="ExternalInput").ap()
    x_own = nc.dram_tensor("x_own", [OWNP, D], F32, kind="ExternalInput").ap()
    wq_d = nc.dram_tensor("wq", [H, D], F32, kind="ExternalInput").ap()
    wk_d = nc.dram_tensor("wk", [H, D], F32, kind="ExternalInput").ap()
    wv_d = nc.dram_tensor("wv", [H, D], F32, kind="ExternalInput").ap()
    wo_d = nc.dram_tensor("wo", [D, H], F32, kind="ExternalInput").ap()
    bq_d = nc.dram_tensor("bq", [1, H], F32, kind="ExternalInput").ap()
    bk_d = nc.dram_tensor("bk", [1, H], F32, kind="ExternalInput").ap()
    bv_d = nc.dram_tensor("bv", [1, H], F32, kind="ExternalInput").ap()
    bo_d = nc.dram_tensor("bo", [1, D], F32, kind="ExternalInput").ap()
    w1_d = nc.dram_tensor("w1", [DH, D], F32, kind="ExternalInput").ap()
    w2_d = nc.dram_tensor("w2", [D, DH], F32, kind="ExternalInput").ap()
    cw_d = nc.dram_tensor("cw", [128, 96], F32, kind="ExternalInput").ap()
    cb_d = nc.dram_tensor("cb", [128, 32], F32, kind="ExternalInput").ap()
    sel_d = nc.dram_tensor("seltab", [1, 64], I32, kind="ExternalInput").ap()
    selw_d = nc.dram_tensor("selw_in", [128, 36], F32, kind="ExternalInput").ap()
    out_d = nc.dram_tensor("out_own", [2 * CH, D], F32, kind="ExternalOutput").ap()

    xnhi_h = nc.dram_tensor("xnhi_h", [KV, D], BF16).ap()
    xnlo_h = nc.dram_tensor("xnlo_h", [KV, D], BF16).ap()
    xq1_h = nc.dram_tensor("xq1_h", [OWNP, D], BF16).ap()
    w1q_h = nc.dram_tensor("w1q_h", [DH, D], BF16).ap()
    w2q_h = nc.dram_tensor("w2q_h", [D, DH], BF16).ap()
    kfm_h = nc.dram_tensor("kfm_h", [2, H, KV], BF16).ap()
    vtm_h = nc.dram_tensor("vtm_h", [2, 72 * H, 132], BF16).ap()
    qall_h = {hl: nc.dram_tensor(f"qall{hl}", [H, 2 * CH], BF16).ap() for hl in (0, 1)}
    part_h = nc.dram_tensor("part_h", [2 * CH, D], F32).ap()

    sb = lambda name, shape, dt=F32: nc.alloc_sbuf_tensor(name, shape, dt)
    ones_f = sb("ones_f", [128, 128])
    ones_b = sb("ones_b", [128, 128], BF16)
    identf = sb("identf", [128, 128])
    identb = sb("identb", [128, 128], BF16)
    tri_T = sb("tri_T", [128, 16 * 128], BF16)
    tri_qk = sb("tri_qk", [128, 16 * 128], BF16)
    wot = {0: sb("wot0", [128, D], BF16), 1: sb("wot1", [128, D], BF16)}
    brow = {nm: sb(f"brow_{nm}", [1, 128]) for nm in ("bq", "bk", "bv")}
    bo_row = sb("bo_row", [1, D])
    cw_sb = sb("cw_sb", [128, 3 * 32])
    cb_sb = sb("cb_sb", [128, 32])
    wsc = sb("wsc", [128, 4])
    sel_sb = nc.alloc_sbuf_tensor("sel_sb", [1, 64], I32)
    s1c = sb("s1c", [128, 18])
    s2c = sb("s2c", [128, 16])
    mAB = sb("mAB", [128, 32])
    selw = sb("selw", [128, 36])
    BFA = sb("BFA", [128, 32768], BF16)
    BFB = sb("BFB", [128, 16384], BF16)
    wqt = {}
    for ii, (nm, hl) in enumerate([(n, h) for n in "qkv" for h in (0, 1)]):
        wqt[(nm, hl)] = BFB[:, ii * 1024:(ii + 1) * 1024]
    FA = sb("FA", [128, 8192], F32)
    trash = tri_T[:, 0:1024]
    xnf = sb("xnf", [128, 1024])
    sm1 = sb("sm1", [128, 1024])
    sm2 = sb("sm2", [128, 1024])
    sm4 = sb("sm4", [128, 1024])
    smi = nc.alloc_sbuf_tensor("smi", [128, 1024], I32)
    evA = sb("evA", [128, 1024])
    bh1 = sb("bh1", [128, 1024], BF16)
    bh2 = sb("bh2", [128, 1024], BF16)
    khi_b = BFB[:, 6144:7168]
    klo_b = BFB[:, 7168:8192]
    bh1b = BFB[:, 8192:9216]
    bh2b = BFB[:, 9216:10240]
    xqf = sb("xqf", [128, 8 * 272], BF16)
    hts = [sb("hts0", [128, 258]), sb("hts1", [128, 258])]
    cv1 = [sb("cv1_0", [128, 256]), sb("cv1_1", [128, 256])]
    cv2 = [sb("cv2_0", [128, 256]), sb("cv2_1", [128, 256])]
    s1bc = sb("s1bc", [128, 384])
    ascbc = s1bc
    M1 = [sb("M1_0", [128, 256]), sb("M1_1", [128, 256])]
    M1n = [sb("M1n_0", [128, 256]), sb("M1n_1", [128, 256])]
    diagt = sb("diagt", [128, 128])
    diagS = xqf[:, 0:1024]
    vtt = {0: sb("vtt0", [128, 132], BF16), 1: sb("vtt1", [128, 132], BF16)}
    nmst = sb("nmst", [128, 8])
    mloc = sb("mloc", [128, 8])
    colsc = sb("colsc", [128, 24])
    ch05 = sb("ch05", [128, 1024])
    dsc = sb("dsc", [128, 64])
    PS = [nc.alloc_psum_tensor(f"PS{i}", [128, 512], F32) for i in range(8)]
    psT = PS[7]

    def mm512(pair, lhsT, rhs1024, start, stop, reads, wkeys):
        for h in range(2):
            p.op("tensor", lambda e, h=h, lhsT=lhsT, rhs1024=rhs1024, start=start, stop=stop: e.matmul(
                pair[h][:], lhsT, rhs1024[:, h * 512:(h + 1) * 512], start=start, stop=stop),
                reads=list(reads), writes=[wkeys[h]])

    p.op("gpsimd", lambda e: e.memset(ones_f[:], 1.0), writes=["ones_f"])
    p.op("gpsimd", lambda e: e.memset(ones512[:], 1.0), writes=["ones512"])
    p.op("gpsimd", lambda e: e.memset(ones_b[:], 1.0), writes=["ones_b"])
    p.op("gpsimd", lambda e: e.affine_select(identf[:], ones_f[:], [[-1, 128]], OP.is_equal,
                                             0.0, channel_multiplier=1),
         reads=["ones_f"], writes=["identf"])
    p.op("vector", lambda e: e.tensor_copy(identb[:], identf[:]), reads=["identf"], writes=["identb"])
    def build_tri_masks():
        p.op("gpsimd", lambda e: e.memset(tri_T[:], 0.0), writes=["tri_T"], fence=["trash"])
        p.op("gpsimd", lambda e: e.memset(tri_qk[:], 0.0), writes=["tri_qk"])
        p.op("gpsimd", lambda e: e.affine_select(tri_T.rearrange("p (c q) -> p c q", c=16),
                                                 tri_T.rearrange("p (c q) -> p c q", c=16),
                                                 [[128, 16], [1, 128]], OP.is_ge, NEG,
                                                 base=-8 * 128, channel_multiplier=-1),
             reads=["tri_T"], writes=["tri_T"])
        p.op("gpsimd", lambda e: e.affine_select(tri_qk.rearrange("p (c k) -> p c k", c=16),
                                                 tri_qk.rearrange("p (c k) -> p c k", c=16),
                                                 [[-128, 16], [-1, 128]], OP.is_ge, NEG,
                                                 base=8 * 128, channel_multiplier=1),
             reads=["tri_qk"], writes=["tri_qk"])
    p.op("sync", lambda e: e.dma_start(out=sel_sb[:], in_=sel_d), writes=["sel_sb"], dma=True)
    p.op("sync", lambda e: e.dma_start(out=selw[:], in_=selw_d), writes=["selw"], dma=True)
    p.op("sync", lambda e: e.dma_start(out=bo_row[:], in_=bo_d), writes=["bo_row"], dma=True)
    for nm, t in (("bq", bq_d), ("bk", bk_d), ("bv", bv_d)):
        p.op("sync", lambda e, t=t, d=brow[nm]: e.dma_start(out=d[:], in_=t),
             writes=[nm], dma=True)
    p.op("sync", lambda e: e.dma_start(out=cw_sb[:], in_=cw_d), writes=["cw_sb"], dma=True)
    p.op("sync", lambda e: e.dma_start(out=cb_sb[:], in_=cb_d), writes=["cb_sb"], dma=True)

    wtmp, whib, wlob = ch05, bh1, bh2

    def small_w_transpose(srcfn, srckey, dsts, dstkeys):
        for c in range(8):
            p.op("tensor", lambda e, c=c: e.transpose(
                psT[:, 0:128], wtmp[:, c * 128:(c + 1) * 128], identf[:]),
                reads=[srckey, "identf"], writes=["PSb7"])
            p.op("scalar", lambda e, d=dsts[0], c=c: e.activation(
                d[:, c * 128:(c + 1) * 128], psT[:, 0:128], AF.Copy),
                reads=["PSb7"], writes=[dstkeys[0]])
            p.op("vector", lambda e, d0=dsts[0], d1=dsts[1], c=c: e.scalar_tensor_tensor(
                d1[:, c * 128:(c + 1) * 128], d0[:, c * 128:(c + 1) * 128], -1.0,
                psT[:, 0:128], OP.mult, OP.add),
                reads=["PSb7", dstkeys[0]], writes=[dstkeys[1]])

    for nm, t in (("q", wq_d), ("k", wk_d), ("v", wv_d)):
        p.op("sync", lambda e, t=t: e.dma_start(out=wtmp[:], in_=t), writes=["ch05"], dma=True)
        small_w_transpose(None, "ch05", {0: wqt[(nm, 0)], 1: wqt[(nm, 1)]},
                          {0: f"wqt_{nm}0", 1: f"wqt_{nm}1"})
    p.op("sync", lambda e: e.dma_start(out=wtmp.rearrange("p (c h) -> p c h", c=8),
                                       in_=wo_d.rearrange("(c p) h -> p c h", p=128)),
         writes=["ch05"], dma=True)
    small_w_transpose(None, "ch05", wot, {0: "wot0", 1: "wot1"})
    p.op("gpsimd", lambda e: e.memset(ch05[:], 0.5), writes=["ch05"])

    qbuf = [sb("qbufA", [128, 512]), sb("qbufB", [128, 512])]
    ones512 = sb("ones512", [1, 512])
    qge = sb("qge", [128, 512], BF16)
    qle = sb("qle", [128, 512], BF16)
    qsub = [sb("qsubA", [128, 512], BF16), sb("qsubB", [128, 512], BF16)]

    def mk_quant_units(wsrc, n_rows, n_cols, dst_h, sc_col, tag):
        ntile = n_rows // 128
        nch = n_cols // 512
        units = []
        units.append(lambda: p.op("vector", lambda e: e.memset(colsc[:, 0:1], 0.0),
                                  writes=["colsc0"], fence=["colsc1"]))
        for i, (t0, ch) in enumerate([(t, c) for t in range(ntile) for c in range(nch)]):
            def em(t0=t0, ch=ch, par=i % 2, q="sync"):
                b = qbuf[par]
                p.op(q, lambda e, t0=t0, ch=ch, b=b: e.dma_start(
                    out=b[:], in_=wsrc[t0 * 128:(t0 + 1) * 128, ch * 512:(ch + 1) * 512]),
                    writes=[f"qbuf{par}"], dma=True)
                p.op("vector", lambda e, b=b: e.tensor_reduce(
                    colsc[:, 1:2], b[:], AX, OP.add, apply_absolute_value=True),
                    reads=[f"qbuf{par}"], writes=["colsc1"])
                p.op("vector", lambda e: e.tensor_tensor(
                    colsc[:, 0:1], colsc[:, 0:1], colsc[:, 1:2], OP.add),
                    reads=["colsc1", "colsc0"], writes=["colsc0"])
            units.append(em)

        def fin():
            p.op("tensor", lambda e: e.matmul(PS[6][0:1, 0:1], ones_f[:, 0:1], colsc[:, 0:1],
                                              start=True, stop=True),
                 reads=["colsc0", "ones_f"], writes=["PSb6"])
            p.op("vector", lambda e, nn=float(n_rows * n_cols): e.tensor_scalar(
                colsc[0:1, 2:3], PS[6][0:1, 0:1], 1.0 / nn, 1e-5, OP.mult, OP.max),
                reads=["PSb6"], writes=["colsc2"])
            p.op("vector", lambda e: e.reciprocal(colsc[0:1, 3:4], colsc[0:1, 2:3]),
                 reads=["colsc2"], writes=["colsc3"])
            p.op("tensor", lambda e: e.matmul(PS[6][:, 1:2], ones_f[0:1, :], colsc[0:1, 3:4],
                                              start=True, stop=True),
                 reads=["colsc3", "ones_f"], writes=["PSb6"])
            p.op("tensor", lambda e: e.matmul(PS[6][:, 2:3], ones_f[0:1, :], colsc[0:1, 2:3],
                                              start=True, stop=True),
                 reads=["colsc2", "ones_f"], writes=["PSb6"])
            p.op("vector", lambda e, sc=sc_col: e.tensor_copy(wsc[:, sc:sc + 1], PS[6][:, 1:2]),
                 reads=["PSb6"], writes=[f"wsc{sc_col}"])
            p.op("vector", lambda e, sc=sc_col: e.tensor_copy(wsc[:, sc + 2:sc + 3], PS[6][:, 2:3]),
                 reads=["PSb6"], writes=[f"wsc{sc_col + 2}"])
        units.append(fin)
        for i, (t0, ch) in enumerate([(t, c) for t in range(ntile) for c in range(nch)]):
            def em2(t0=t0, ch=ch, par=i % 2, q="sync"):
                b = qbuf[par]
                p.op(q, lambda e, t0=t0, ch=ch, b=b: e.dma_start(
                    out=b[:], in_=wsrc[t0 * 128:(t0 + 1) * 128, ch * 512:(ch + 1) * 512]),
                    writes=[f"qbuf{par}"], dma=True)
                p.op("vector", lambda e, b=b, sc=sc_col: e.tensor_scalar(
                    qge[:], b[:], wsc[:, sc:sc + 1], 0.5, OP.mult, OP.is_ge),
                    reads=[f"qbuf{par}", f"wsc{sc_col}"], writes=["qge"])
                p.op("vector", lambda e, b=b, sc=sc_col: e.tensor_scalar(
                    qle[:], b[:], wsc[:, sc:sc + 1], -0.5, OP.mult, OP.is_le),
                    reads=[f"qbuf{par}", f"wsc{sc_col}"], writes=["qle"])
                p.op("vector", lambda e, par=par: e.tensor_tensor(
                    qsub[par][:], qle[:], qge[:], OP.subtract),
                    reads=["qge", "qle"], writes=[f"qsub{par}"])
                p.op("sync", lambda e, t0=t0, ch=ch, par=par: e.dma_start(
                    out=dst_h[t0 * 128:(t0 + 1) * 128, ch * 512:(ch + 1) * 512],
                    in_=qsub[par][:]),
                    reads=[f"qsub{par}"], writes=[f"wqH_{tag}"], dma=True)
            units.append(em2)
        return units

    xb8 = [FA[:, i * 1024:(i + 1) * 1024] for i in range(8)]

    def ln_loads(xsrc, t0, half, nt):
        for i in range(nt):
            ti = half * 4 + i
            p.op("gpsimd", lambda e, row=(t0 + i) * 128, ti=ti, xsrc=xsrc: e.dma_start(
                out=xb8[ti], in_=xsrc[row:row + 128, :]),
                writes=[f"lnx_{ti}"], dma=True, fence=["FA", "FA2"])

    def ln_stats4(half, nt):
        K = f"dsc{half}"
        for i in range(nt):
            ti = half * 4 + i
            p.op("scalar", lambda e, ti=ti: e.activation(trash[:], xb8[ti][:], AF.Copy,
                                                         accum_out=dsc[:, ti:ti + 1]),
                 reads=[f"lnx_{ti}"], writes=[K, "trash"])
            p.op("scalar", lambda e, ti=ti: e.activation(trash[:], xb8[ti][:], AF.Square,
                                                         accum_out=dsc[:, 8 + ti:9 + ti]),
                 reads=[f"lnx_{ti}"], writes=[K, "trash"])

    def ln_chain4(half, nt):
        c = dsc
        K = f"dsc{half}"
        b = half * 4
        sm, sv, st = 32 + half * 16, 36 + half * 16, 40 + half * 16

        def v(fn):
            p.op("vector", fn, reads=[K], writes=[K])
        v(lambda e: e.tensor_scalar(c[:, sm:sm + nt], c[:, b:b + nt], 1.0 / D, None, OP.mult))
        v(lambda e: e.tensor_tensor(c[:, sv:sv + nt], c[:, sm:sm + nt], c[:, sm:sm + nt], OP.mult))
        v(lambda e: e.scalar_tensor_tensor(c[:, sv:sv + nt], c[:, 8 + b:8 + b + nt], 1.0 / D,
                                           c[:, sv:sv + nt], OP.mult, OP.subtract))
        v(lambda e: e.tensor_scalar(c[:, sv:sv + nt], c[:, sv:sv + nt], 1e-5, None, OP.add))
        p.op("scalar", lambda e: e.activation(c[:, st:st + nt], c[:, sv:sv + nt], AF.Sqrt),
             reads=[K], writes=[K])
        v(lambda e: e.reciprocal(c[:, 16 + b:16 + b + nt], c[:, st:st + nt]))
        v(lambda e: e.tensor_tensor(c[:, st:st + nt], c[:, 16 + b:16 + b + nt],
                                    c[:, 16 + b:16 + b + nt], OP.mult))
        v(lambda e: e.tensor_tensor(c[:, st:st + nt], c[:, st:st + nt], c[:, sv:sv + nt], OP.mult))
        v(lambda e: e.tensor_scalar(c[:, st:st + nt], c[:, st:st + nt], -0.5, 1.5, OP.mult, OP.add))
        v(lambda e: e.tensor_tensor(c[:, 16 + b:16 + b + nt], c[:, 16 + b:16 + b + nt],
                                    c[:, st:st + nt], OP.mult))
        v(lambda e: e.tensor_tensor(c[:, 24 + b:24 + b + nt], c[:, sm:sm + nt],
                                    c[:, 16 + b:16 + b + nt], OP.mult))
        v(lambda e: e.tensor_scalar(c[:, 24 + b:24 + b + nt], c[:, 24 + b:24 + b + nt],
                                    -1.0, None, OP.mult))

    lnpair = {0: (bh1, bh2, "bh1", "bh2"), 1: (bh1b, bh2b, "bh1B", "bh2B")}

    def ln_apply4(ti, half):
        K = f"dsc{half}"
        hb, lb, hk, lk = lnpair[ti % 2]
        p.op("vector", lambda e, ti=ti: e.tensor_scalar(xnf[:], xb8[ti][:], dsc[:, 16 + ti:17 + ti],
                                                        dsc[:, 24 + ti:25 + ti], OP.mult, OP.add),
             reads=[f"lnx_{ti}", K], writes=["xnf"])
        p.op("vector", lambda e, hb=hb: e.tensor_copy(hb[:], xnf[:]),
             reads=["xnf"], writes=[hk])
        p.op("vector", lambda e, hb=hb, lb=lb: e.scalar_tensor_tensor(lb[:], hb[:], -1.0, xnf[:], OP.mult, OP.add),
             reads=[hk, "xnf"], writes=[lk])

    def kv_post(t):
        hb, lb, hk, lk = lnpair[t % 2]
        p.op("gpsimd", lambda e, t=t, hb=hb: e.dma_start(out=xnhi_h[t * 128:(t + 1) * 128, :], in_=hb[:]),
             reads=[hk], writes=[f"xnhiH_{t}"], dma=True)
        p.op("gpsimd", lambda e, t=t, lb=lb: e.dma_start(out=xnlo_h[t * 128:(t + 1) * 128, :], in_=lb[:]),
             reads=[lk], writes=[f"xnloH_{t}"], dma=True)

    def own_post(t):
        p.op("vector", lambda e: e.tensor_reduce(colsc[:, 14:15], xnf[:], AX, OP.max,
                                                 apply_absolute_value=True),
             reads=["xnf"], writes=["colscq"])
        p.op("vector", lambda e: e.tensor_scalar(colsc[:, 15:16], colsc[:, 14:15], 1e-5, None, OP.max),
             reads=["colscq"], writes=["colscq"])
        p.op("vector", lambda e: e.reciprocal(colsc[:, 16:17], colsc[:, 15:16]),
             reads=["colscq"], writes=["colscq"])
        p.op("vector", lambda e: e.tensor_scalar(colsc[:, 17:18], colsc[:, 16:17], 127.0, None, OP.mult),
             reads=["colscq"], writes=["colscq"])
        p.op("vector", lambda e, t=t: e.scalar_tensor_tensor(
            s1c[:, t:t + 1], colsc[:, 15:16], 1.0 / 127.0, wsc[:, 2:3], OP.mult, OP.mult),
            reads=["colscq", "wsc2"], writes=["s1c"])
        p.op("vector", lambda e: e.tensor_scalar(sm1[:], xnf[:], colsc[:, 17:18], None, OP.mult),
             reads=["xnf", "colscq"], writes=["sm1"])
        p.op("vector", lambda e: e.tensor_scalar(sm2[:], xnf[:], 0.0, 0.5, OP.is_ge, OP.subtract),
             reads=["xnf"], writes=["sm2"])
        p.op("vector", lambda e: e.tensor_tensor(smi[:], sm1[:], sm2[:], OP.add),
             reads=["sm1", "sm2"], writes=["smi"])
        p.op("vector", lambda e: e.tensor_tensor(sm4[:], sm1[:], smi[:], OP.subtract),
             reads=["smi", "sm1"], writes=["sm4"])
        p.op("vector", lambda e: e.tensor_tensor(sm2[:], sm4[:], ch05[:], OP.is_gt),
             reads=["sm4", "ch05"], writes=["sm2"])
        p.op("vector", lambda e: e.tensor_tensor(sm2[:], sm2[:], smi[:], OP.add),
             reads=["sm2", "smi"], writes=["sm2"])
        hb, lb, hk, lk = lnpair[t % 2]
        p.op("vector", lambda e, hb=hb: e.scalar_tensor_tensor(hb[:], sm4[:], -0.5, sm2[:], OP.is_lt, OP.subtract),
             reads=["sm4", "sm2"], writes=[hk])
        p.op("gpsimd", lambda e, t=t, hb=hb: e.dma_start(out=xq1_h[t * 128:(t + 1) * 128, :], in_=hb[:]),
             reads=[hk], writes=[f"xq1H_{t}"], dma=True)

    def ln_body(xsrc, t0, half, nt, post):
        ln_stats4(half, nt)
        ln_chain4(half, nt)
        for i in range(nt):
            ln_apply4(half * 4 + i, half)
            post(t0 + i)

    _lnq = {"pending": None, "half": 0}

    def sched_group(xsrc, t0, nt, post):
        h = _lnq["half"]
        _lnq["half"] ^= 1
        ln_loads(xsrc, t0, h, nt)
        prev = _lnq["pending"]
        _lnq["pending"] = (xsrc, t0, h, nt, post)
        if prev is not None:
            ln_body(*prev)

    def sched_flush():
        prev = _lnq["pending"]
        _lnq["pending"] = None
        if prev is not None:
            ln_body(*prev)

    xnfm = {(c, hl): BFA[:, (hl * 8 + c) * 1024:(hl * 8 + c + 1) * 1024]
            for c in range(8) for hl in (0, 1)}
    p.op("gpsimd", lambda e: e.memset(vtt[0][:, 128:132], 0.0), writes=["vtt0c"])
    p.op("gpsimd", lambda e: e.memset(vtt[0][:, 128:129], 1.0), writes=["vtt0c"], fence=["vtt0c"])
    p.op("gpsimd", lambda e: e.memset(vtt[1][:, 128:132], 0.0), writes=["vtt1c"])
    PQ = (PS[0], PS[1])
    PK = (PS[2], PS[3])
    PV_ = (PS[4], PS[5])
    pkeys = {id(PS[i]): f"PSb{i}" for i in range(8)}

    def qkv_block(b):
        xn_keys = []
        for c in range(8):
            for hl, src in ((0, xnhi_h), (1, xnlo_h)):
                kk = f"xnfm_{c}_{hl}"
                xn_keys.append(kk)
                p.op("sync", lambda e, c=c, src=src, b=b, hl=hl: e.dma_start(
                    out=xnfm[(c, hl)], in_=src[b * CH:(b + 1) * CH, c * 128:(c + 1) * 128],
                    transpose=True),
                    reads=[(f"xnhiH_{b * 8 + i}" if hl == 0 else f"xnloH_{b * 8 + i}") for i in range(8)],
                    writes=[kk], dma=True, fence=["BFA"])
        for nm, pair, brk in (("q", PQ, "bq"), ("k", PK, "bk"), ("v", PV_, "bv")):
            for hh in range(2):
                p.op("tensor", lambda e, pair=pair, nm=nm, hh=hh, brk=brk: e.matmul(
                    pair[hh][:], brow[brk][0:1, :], ones512[0:1, :],
                    start=True, stop=False),
                    reads=[brk, "ones512"], writes=[pkeys[id(pair[hh])]])
            passes = [(0, 0), (1, 0), (0, 1)]
            for pi, (whl, xhl) in enumerate(passes):
                for c in range(8):
                    sp = (pi == 2 and c == 7)
                    for hh in range(2):
                        p.op("tensor", lambda e, pair=pair, nm=nm, whl=whl, xhl=xhl, c=c, sp=sp, hh=hh: e.matmul(
                            pair[hh][:], wqt[(nm, whl)][:, c * 128:(c + 1) * 128],
                            xnfm[(c, xhl)][:, hh * 512:(hh + 1) * 512], start=False, stop=sp),
                            reads=[f"wqt_{nm}{whl}", f"xnfm_{c}_{xhl}"],
                            writes=[pkeys[id(pair[hh])]])
        tag = f"qkv{b}"
        for nm, pair in (("k", PK), ("v", PV_), ("q", PQ)):
            if nm != "v":
                for hh in range(2):
                    p.op("scalar", lambda e, pair=pair, hh=hh: e.activation(
                        khi_b[:, hh * 512:(hh + 1) * 512], pair[hh][:], AF.Copy),
                        reads=[pkeys[id(pair[hh])]], writes=["khiB"])
                    p.op("vector", lambda e, pair=pair, hh=hh: e.scalar_tensor_tensor(
                        klo_b[:, hh * 512:(hh + 1) * 512], khi_b[:, hh * 512:(hh + 1) * 512],
                        -1.0, pair[hh][:], OP.mult, OP.add),
                        reads=["khiB", pkeys[id(pair[hh])]], writes=["kloB"])
            else:
                for hh in range(2):
                    p.op("scalar", lambda e, pair=pair, hh=hh: e.activation(
                        evA[:, hh * 512:(hh + 1) * 512], pair[hh][:], AF.Copy),
                        reads=[pkeys[id(pair[hh])]], writes=["evA"])
            if nm == "k":
                p.op("sync", lambda e, b=b: e.dma_start(out=kfm_h[0, :, b * CH:(b + 1) * CH], in_=khi_b[:]),
                     reads=["khiB"], writes=[f"kfmH0_{b}"], dma=True)
                p.op("sync", lambda e, b=b: e.dma_start(out=kfm_h[1, :, b * CH:(b + 1) * CH], in_=klo_b[:]),
                     reads=["kloB"], writes=[f"kfmH1_{b}"], dma=True)
            elif nm == "v":
                for i in range(8):
                    p.op("tensor", lambda e, i=i: e.transpose(
                        psT[:, 0:128], evA[:, i * 128:(i + 1) * 128], identf[:]),
                        reads=["evA", "identf"], writes=["PSb7"])
                    p.op("scalar", lambda e: e.activation(vtt[0][:, 0:128], psT[:, 0:128], AF.Copy),
                         reads=["PSb7"], writes=["vtt0"], fence=["vtt0c"])
                    p.op("vector", lambda e: e.scalar_tensor_tensor(
                        vtt[1][:, 0:128], vtt[0][:, 0:128], -1.0, psT[:, 0:128], OP.mult, OP.add),
                        reads=["PSb7", "vtt0"], writes=["vtt1"], fence=["vtt1c"])
                    for hl in (0, 1):
                        p.op("sync", lambda e, hl=hl, b=b, i=i: e.dma_start(
                            out=vtm_h[hl, (b * 8 + i) * 128:(b * 8 + i + 1) * 128, :], in_=vtt[hl][:]),
                            reads=[f"vtt{hl}", f"vtt{hl}c"], writes=[f"vtmH_{hl}_{b}_{i}"], dma=True)
            else:
                if b >= 7:
                    p.op("sync", lambda e, b=b: e.dma_start(
                        out=qall_h[0][:, (b - 7) * CH:(b - 6) * CH], in_=khi_b[:]),
                        reads=["khiB"], writes=[f"qallH0_{b}"], dma=True)
                    p.op("sync", lambda e, b=b: e.dma_start(
                        out=qall_h[1][:, (b - 7) * CH:(b - 6) * CH], in_=klo_b[:]),
                        reads=["kloB"], writes=[f"qallH1_{b}"], dma=True)

    uq = (mk_quant_units(w1_d, DH, D, w1q_h, 0, "w1")
          + mk_quant_units(w2_d, D, DH, w2q_h, 1, "w2"))
    kv_order = [7, 8, 0, 1, 2, 3, 4, 5, 6]
    own_groups = [(0, 4), (4, 4), (8, 4), (12, 4), (16, 2)]
    per = (len(uq) + 8) // 9
    for idx, b in enumerate(kv_order):
        sched_group(x_kv, b * 8, 4, kv_post)
        sched_group(x_kv, b * 8 + 4, 4, kv_post)
        if idx >= 4:
            og = own_groups[idx - 4]
            sched_group(x_own, og[0], og[1], own_post)
        if idx >= 1:
            qkv_block(kv_order[idx - 1])
        for u in uq[idx * per:(idx + 1) * per]:
            u()
    sched_flush()
    qkv_block(kv_order[-1])
    for u in uq[9 * per:]:
        u()

    qall_done = [f"qallH{hl}_{b}" for hl in (0, 1) for b in (7, 8)]
    kv_keys = [f"kfmH{hl}_{b}" for hl in (0, 1) for b in range(NBLK)]
    vt_keys = [f"vtmH_{hl}_{b}_{i}" for hl in (0, 1) for b in range(NBLK) for i in range(8)]

    w1t_keys = []
    for c in range(8):
        kk = f"W1T_{c}"
        w1t_keys.append(kk)
        p.op("sync", lambda e, c=c: e.dma_start(
            out=BFA[:, c * 4096:(c + 1) * 4096], in_=w1q_h[:, c * 128:(c + 1) * 128],
            transpose=True),
            reads=["wqH_w1"], writes=[kk], dma=True,
            fence=[f"xnfm_{cc}_{hl}" for cc in range(8) for hl in (0, 1)])
    g_all = FA
    nq_sub = BFB[:, 0:8192]
    w2qt = BFB[:, 8192:16384]

    ascb = {0: vtt[0], 1: vtt[1]}

    def ffn_F(cc, sub, qprev=False):
        r0 = cc * GRID + sub * 256
        for c in range(8):
            p.op("sync", lambda e, c=c, r0=r0: e.dma_start(
                out=xqf[:, c * 272:(c + 1) * 272],
                in_=xq1_h[r0:r0 + 272, c * 128:(c + 1) * 128], transpose=True),
                reads=[f"xq1H_{r0 // 128 + i}" for i in range(3)],
                writes=["xqf"], dma=True)
        for i in range(3):
            tl = cc * 9 + 2 * sub + i
            p.op("vector", lambda e, tl=tl: e.tensor_tensor(
                diagt[:], identf[:], s1c[:, tl:tl + 1].broadcast_to([128, 128]), OP.mult),
                reads=["s1c", "identf"], writes=["diagt"])
            p.op("tensor", lambda e: e.matmul(PS[1][:, 0:128], ones_f[:], diagt[:],
                                              start=True, stop=True),
                 reads=["diagt", "ones_f"], writes=["PSb1"])
            p.op("vector", lambda e, i=i: e.tensor_copy(s1bc[:, i * 128:(i + 1) * 128],
                                                        PS[1][:, 0:128]),
                 reads=["PSb1"], writes=["s1bc"])
        for par in range(2):
            p.op("gpsimd", lambda e, par=par: e.memset(M1[par][:], 0.0), writes=[f"M1_{par}"],
                 fence=["FA2"])
            p.op("gpsimd", lambda e, par=par: e.memset(M1n[par][:], 0.0), writes=[f"M1n_{par}"])

        def mm_minmax(hid):
            par = hid % 2
            p.op("vector", lambda e, hid=hid, par=par: e.tensor_tensor(
                M1[par][:], M1[par][:], g_all[:, hid * 256:(hid + 1) * 256], OP.max),
                reads=[f"gq_{hid // 4}", f"M1_{par}"], writes=[f"M1_{par}"])
            p.op("vector", lambda e, hid=hid, par=par: e.tensor_tensor(
                M1n[par][:], M1n[par][:], g_all[:, hid * 256:(hid + 1) * 256], OP.min),
                reads=[f"gq_{hid // 4}", f"M1n_{par}"], writes=[f"M1n_{par}"])

        for hid in range(32):
            if qprev and hid % 4 == 0:
                ffn_Q_unit(hid // 4)
            par = hid % 2
            psh = PS[0] if par == 0 else PS[6]
            pshk = "PSb0" if par == 0 else "PSb6"
            for c in range(8):
                p.op("tensor", lambda e, c=c, hid=hid, psh=psh: e.matmul(
                    psh[:, 0:258], BFA[:, c * 4096 + hid * 128:c * 4096 + (hid + 1) * 128],
                    xqf[:, c * 272:c * 272 + 258], start=(c == 0), stop=(c == 7)),
                    reads=["xqf", f"W1T_{c}"], writes=[pshk])
            p.op("vector", lambda e, par=par, psh=psh: e.tensor_tensor(
                hts[par][:], psh[:, 0:258], s1bc[:, 0:258], OP.mult),
                reads=[pshk, "s1bc"], writes=[f"hts{par}"])
            p.op("vector", lambda e, hid=hid, par=par: e.tensor_scalar(
                cv1[par][:], hts[par][:, 1:257], cw_sb[:, 32 + hid:33 + hid], None, OP.mult),
                reads=[f"hts{par}", "cw_sb"], writes=[f"cv1_{par}"])
            p.op("vector", lambda e, hid=hid, par=par: e.tensor_scalar(
                cv2[par][:], hts[par][:, 0:256], cw_sb[:, hid:hid + 1], None, OP.mult),
                reads=[f"hts{par}", "cw_sb"], writes=[f"cv2_{par}"])
            p.op("vector", lambda e, par=par: e.tensor_tensor(cv2[par][:], cv2[par][:], cv1[par][:], OP.add),
                reads=[f"cv2_{par}", f"cv1_{par}"], writes=[f"cv2_{par}"])
            p.op("vector", lambda e, hid=hid, par=par: e.scalar_tensor_tensor(
                cv1[par][:], hts[par][:, 2:258], cw_sb[:, 64 + hid:65 + hid], cv2[par][:],
                OP.mult, OP.add),
                reads=[f"hts{par}", f"cv2_{par}", "cw_sb"], writes=[f"cv1_{par}"])
            lnf = ([f"lnx_{hid // 4}"] if (cc, sub) == (0, 0) and hid % 4 == 0 else [])
            p.op("scalar", lambda e, hid=hid, par=par: e.activation(
                g_all[:, hid * 256:(hid + 1) * 256], cv1[par][:], AF.Silu,
                bias=cb_sb[:, hid:hid + 1]),
                reads=[f"cv1_{par}", "cb_sb"], writes=[f"gq_{hid // 4}"], fence=lnf)
            if hid > 0:
                mm_minmax(hid - 1)
        mm_minmax(31)

    def ffn_M(cc, sub):
        p.op("vector", lambda e: e.tensor_tensor(M1[0][:], M1[0][:], M1[1][:], OP.max),
             reads=["M1_0", "M1_1"], writes=["M1_0"])
        p.op("vector", lambda e: e.tensor_tensor(M1n[0][:], M1n[0][:], M1n[1][:], OP.min),
             reads=["M1n_0", "M1n_1"], writes=["M1n_0"])
        for i in range(2):
            p.op("tensor", lambda e, i=i: e.transpose(psT[:, 0:128], M1[0][:, i * 128:(i + 1) * 128],
                                                      identf[:]),
                 reads=["M1_0", "identf"], writes=["PSb7"])
            p.op("vector", lambda e: e.tensor_reduce(colsc[:, 4:5], psT[:, 0:128], AX, OP.max),
                 reads=["PSb7"], writes=["colsc45"])
            p.op("tensor", lambda e, i=i: e.transpose(psT[:, 0:128], M1n[0][:, i * 128:(i + 1) * 128],
                                                      identf[:]),
                 reads=["M1n_0", "identf"], writes=["PSb7"], fence=["colsc45"])
            p.op("vector", lambda e: e.tensor_reduce(colsc[:, 3:4], psT[:, 0:128], AX, OP.min,
                                                     negate=True),
                 reads=["PSb7"], writes=["colsc34"])
            p.op("vector", lambda e: e.tensor_tensor(colsc[:, 4:5], colsc[:, 4:5], colsc[:, 3:4],
                                                     OP.max),
                 reads=["colsc34", "colsc45"], writes=["colsc45"])
            p.op("vector", lambda e: e.tensor_scalar(colsc[:, 5:6], colsc[:, 4:5], 1e-5, None, OP.max),
                 reads=["colsc45"], writes=["colsc56"])
            p.op("vector", lambda e: e.reciprocal(colsc[:, 6:7], colsc[:, 5:6]),
                 reads=["colsc56"], writes=["colsc67"])
            p.op("vector", lambda e, i=i: e.tensor_scalar(colsc[:, 7 + i:8 + i], colsc[:, 6:7],
                                                          127.0, None, OP.mult),
                 reads=["colsc67"], writes=["colsc78"])
            tl = cc * 8 + 2 * sub + i
            p.op("vector", lambda e, tl=tl: e.scalar_tensor_tensor(
                s2c[:, tl:tl + 1], colsc[:, 5:6], 1.0 / 127.0, wsc[:, 3:4], OP.mult, OP.mult),
                reads=["colsc56", "wsc3"], writes=["s2c"])
            p.op("vector", lambda e, i=i: e.tensor_tensor(
                diagt[:], identf[:], colsc[:, 7 + i:8 + i].broadcast_to([128, 128]), OP.mult),
                reads=["colsc78", "identf"], writes=["diagt"])
            p.op("tensor", lambda e: e.matmul(PS[1][:, 0:128], ones_f[:], diagt[:],
                                              start=True, stop=True),
                 reads=["diagt", "ones_f"], writes=["PSb1"])
            p.op("vector", lambda e, i=i: e.tensor_copy(ascb[i][:, 0:128], PS[1][:, 0:128]),
                 reads=["PSb1"], writes=[f"vtt{i}"])

    def ffn_Q_unit(gidx):
            gk = f"gq_{gidx}"
            gsl3 = g_all[:, gidx * 1024:(gidx + 1) * 1024].rearrange("p (g t) -> p g t", g=4)
            gsl = g_all[:, gidx * 1024:(gidx + 1) * 1024]
            p.op("vector", lambda e, gsl=gsl: e.tensor_scalar(
                sm2[:], gsl, 0.0, 0.5, OP.is_ge, OP.subtract),
                reads=[gk], writes=["sm2"])
            for i in range(2):
                a3 = ascb[i][:, 0:128].rearrange("p t -> p () t").broadcast_to([128, 4, 128])
                p.op("vector", lambda e, gsl3=gsl3, a3=a3, i=i: e.tensor_tensor(
                    gsl3[:, :, i * 128:(i + 1) * 128], gsl3[:, :, i * 128:(i + 1) * 128],
                    a3, OP.mult),
                    reads=[f"vtt{i}", gk], writes=[gk])
            p.op("vector", lambda e, gsl=gsl: e.tensor_tensor(smi[:], gsl, sm2[:], OP.add),
                 reads=[gk, "sm2"], writes=["smi"])
            p.op("vector", lambda e, gsl=gsl: e.tensor_tensor(sm4[:], gsl, smi[:], OP.subtract),
                reads=["smi", gk], writes=["sm4"])
            p.op("vector", lambda e: e.tensor_tensor(
                sm2[:], sm4[:], ch05[:], OP.is_gt),
                reads=["sm4", "ch05"], writes=["sm2"])
            p.op("vector", lambda e: e.tensor_tensor(sm2[:], sm2[:], smi[:], OP.add),
                reads=["sm2", "smi"], writes=["sm2"])
            p.op("vector", lambda e, gidx=gidx: e.scalar_tensor_tensor(
                nq_sub[:, gidx * 1024:(gidx + 1) * 1024], sm4[:], -0.5, sm2[:],
                OP.is_lt, OP.subtract),
                reads=["sm4", "sm2"], writes=["nq"],
                fence=[f"wqt_{n}{h}" for n in 'qkv' for h in (0, 1)] + ["khiB", "kloB"])

    def ffn_W(cc, sub):
        for qtr in range(4):
            for hh in range(8):
                hid = qtr * 8 + hh
                p.op("sync", lambda e, hid=hid, hh=hh: e.dma_start(
                    out=w2qt[:, hh * 1024:(hh + 1) * 1024],
                    in_=w2q_h[:, hid * 128:(hid + 1) * 128], transpose=True),
                    reads=["wqH_w2"], writes=["w2qt"], dma=True,
                    fence=["bh1B", "bh2B"])
            for tt in range(2):
                pair = (PS[2], PS[3]) if tt == 0 else (PS[4], PS[5])
                for hh in range(8):
                    for half in range(2):
                        p.op("tensor", lambda e, pair=pair, tt=tt, hh=hh, half=half, qtr=qtr: e.matmul(
                            pair[half][:],
                            nq_sub[:, (qtr * 8 + hh) * 256 + tt * 128:(qtr * 8 + hh) * 256 + tt * 128 + 128],
                            w2qt[:, hh * 1024 + half * 512:hh * 1024 + (half + 1) * 512],
                            start=(qtr == 0 and hh == 0), stop=(qtr == 3 and hh == 7)),
                            reads=["nq", "w2qt"], writes=[pkeys[id(pair[half])]])
        for tt in range(2):
            pair = (PS[2], PS[3]) if tt == 0 else (PS[4], PS[5])
            own0 = cc * CH + sub * 256 + tt * 128
            grid_r = cc * GRID + sub * 256 + tt * 128 + 1
            p.op("sync", lambda e, grid_r=grid_r: e.dma_start(
                out=evA[:], in_=x_own[grid_r:grid_r + 128, :]),
                writes=["evA"], dma=True)
            tl = cc * 8 + sub * 2 + tt
            for half in range(2):
                p.op("vector", lambda e, pair=pair, half=half, tl=tl: e.scalar_tensor_tensor(
                    sm4[:, half * 512:(half + 1) * 512], pair[half][:], s2c[:, tl:tl + 1],
                    evA[:, half * 512:(half + 1) * 512], OP.mult, OP.add),
                    reads=[pkeys[id(pair[half])], "s2c", "evA"], writes=["sm4"])
            p.op("gpsimd", lambda e, own0=own0: e.dma_start(
                out=part_h[own0:own0 + 128, :], in_=sm4[:]),
                reads=["sm4"], writes=[f"partH_{own0}"], dma=True)

    fsubs = [(cc, sub) for cc in range(2) for sub in range(4)]
    ffn_F(*fsubs[0])
    ffn_M(*fsubs[0])
    for k in range(1, 8):
        ffn_F(*fsubs[k], qprev=True)
        ffn_W(*fsubs[k - 1])
        ffn_M(*fsubs[k])
    for gidx in range(8):
        ffn_Q_unit(gidx)
    ffn_W(*fsubs[7])

    part_keys = [f"partH_{o}" for o in range(0, 2 * CH, 128)]

    build_tri_masks()
    qs_hi = BFB[:, 0:1024]
    qs_lo = BFB[:, 1024:2048]
    ks_hi = BFB[:, 2048:3072]
    ks_lo = BFB[:, 3072:4096]
    vt_hi = BFB[:, 4096:5152]
    vt_lo = BFB[:, 5152:6208]
    pT_hi = BFA[:, 0:8192]
    pT_lo = BFA[:, 8192:16384]
    O_loc = FA[:, 0:1056]
    O_acc = {0: FA[:, 1056:2112], 1: FA[:, 2112:3168]}
    pf = evA

    import bass_rust as _br2
    slot_rq = []
    for s in range(NBLK):
        hq = nc.gpsimd.alloc_register(f"rq{s}")
        slot_rq.append((hq, _br2.RuntimeValue(_br2.RegisterHandles([hq]))))

    def load_slot_regs(e):
        last = None
        for s in range(NBLK):
            insts = nc.regs_load(_br2.RegisterHandles([slot_rq[s][0]]), sel_sb[0:1, 18 + s:19 + s])
            if insts:
                last = insts[-1]
        return last
    p.op("gpsimd", load_slot_regs, reads=["sel_sb"], writes=["slotregs"])

    p.op("gpsimd", lambda e: e.memset(mAB[:, 0:16], NEG), writes=["mAB"])
    p.op("gpsimd", lambda e: e.memset(O_loc[:], 0.0), writes=["FA"],
         fence=[f"gq_{i}" for i in range(8)])
    p.op("gpsimd", lambda e: e.memset(O_acc[0][:], 0.0), writes=["FA"])
    p.op("gpsimd", lambda e: e.memset(O_acc[1][:], 0.0), writes=["FA"])

    bfb_fence = ["nq", "w2qt"]
    bfa_fence = list(w1t_keys)

    def slot_loads(s, need_v):
        rq = slot_rq[s][1]
        p.op("gpsimd", lambda e, rq=rq: e.dma_start(out=qs_hi, in_=qall_h[0][:, DS(rq, CH)]),
             reads=qall_done + ["slotregs"], writes=["qs_hi"], dma=True, fence=bfb_fence)
        p.op("gpsimd", lambda e, rq=rq: e.dma_start(out=qs_lo, in_=qall_h[1][:, DS(rq, CH)]),
             reads=qall_done + ["slotregs"], writes=["qs_lo"], dma=True, fence=bfb_fence)
        p.op("sync", lambda e, s=s: e.dma_start(out=ks_hi, in_=kfm_h[0, :, s * CH:(s + 1) * CH]),
             reads=kv_keys, writes=["ks_hi"], dma=True, fence=bfb_fence)
        p.op("sync", lambda e, s=s: e.dma_start(out=ks_lo, in_=kfm_h[1, :, s * CH:(s + 1) * CH]),
             reads=kv_keys, writes=["ks_lo"], dma=True, fence=bfb_fence)
        if need_v:
            for hl, dst, dk in ((0, vt_hi, "vt_hi"), (1, vt_lo, "vt_lo")):
                p.op("sync", lambda e, hl=hl, dst=dst, s=s: e.dma_start(
                    out=dst.rearrange("p (i c) -> p i c", i=8),
                    in_=vtm_h[hl, :, :].rearrange("(i p) c -> p i c", p=128)[:, s * 8:(s + 1) * 8]),
                    reads=vt_keys, writes=[dk], dma=True, fence=bfb_fence)

    for s in range(NBLK):
        slot_loads(s, need_v=False)
        for qt in range(8):
            pair = (PS[0], PS[1]) if qt % 2 == 0 else (PS[2], PS[3])
            kvar = ((qs_hi, "qs_hi", ks_hi, "ks_hi"),
                    (qs_lo, "qs_lo", ks_hi, "ks_hi"),
                    (qs_hi, "qs_hi", ks_lo, "ks_lo"))
            for i, (qb, qk, kb, kk) in enumerate(kvar):
                for hh in range(2):
                    p.op("tensor", lambda e, pair=pair, qb=qb, kb=kb, qt=qt, i=i, hh=hh: e.matmul(
                        pair[hh][:], qb[:, qt * 128:(qt + 1) * 128],
                        kb[:, hh * 512:(hh + 1) * 512], start=(i == 0), stop=(i == 2)),
                        reads=[qk, kk], writes=[pkeys[id(pair[hh])]])
            for hh in range(2):
                if s >= 7:
                    p.op("vector", lambda e, pair=pair, qt=qt, hh=hh: e.tensor_tensor(
                        pair[hh][:], pair[hh][:],
                        tri_qk[:, (8 - qt) * 128 + hh * 512:(8 - qt) * 128 + (hh + 1) * 512], OP.add),
                        reads=[pkeys[id(pair[hh])], "tri_qk"], writes=[pkeys[id(pair[hh])]])
                p.op("vector", lambda e, pair=pair, hh=hh: e.tensor_reduce(
                    colsc[:, 9 + hh:10 + hh], pair[hh][:], AX, OP.max),
                    reads=[pkeys[id(pair[hh])]], writes=["colsc9A"])
            p.op("vector", lambda e, qt=qt: e.tensor_tensor(
                mloc[:, qt:qt + 1], colsc[:, 9:10], colsc[:, 10:11], OP.max),
                reads=["colsc9A"], writes=["mloc"])
        for side in range(2):
            p.op("vector", lambda e, s=s, side=side: e.scalar_tensor_tensor(
                nmst[:], mloc[:, 0:8], selw[:, side * 9 + s:side * 9 + s + 1],
                selw[:, 18 + side * 9 + s:18 + side * 9 + s + 1].broadcast_to([128, 8]),
                OP.mult, OP.add),
                reads=["mloc", "selw"], writes=["nmst"])
            p.op("vector", lambda e, side=side: e.tensor_tensor(
                mAB[:, side * 8:(side + 1) * 8], mAB[:, side * 8:(side + 1) * 8], nmst[:], OP.max),
                reads=["nmst", "mAB"], writes=["mAB"])

    p.op("vector", lambda e: e.tensor_scalar(mAB[:, 16:32], mAB[:, 0:16], -1.0, None, OP.mult),
         reads=["mAB"], writes=["mAB"])

    for s in range(NBLK):
        slot_loads(s, need_v=True)
        p.op("vector", lambda e, s=s: e.tensor_scalar(
            nmst[:], mAB[:, 16:24], selw[:, s:s + 1], None, OP.mult),
            reads=["mAB", "selw"], writes=["nmst"])
        p.op("vector", lambda e, s=s: e.scalar_tensor_tensor(
            nmst[:], mAB[:, 24:32], selw[:, 9 + s:10 + s], nmst[:], OP.mult, OP.add),
            reads=["mAB", "selw", "nmst"], writes=["nmst"])
        for qt in range(8):
            p.op("vector", lambda e, qt=qt: e.tensor_tensor(
                diagS[:, qt * 128:(qt + 1) * 128], identb[:],
                nmst[:, qt:qt + 1].broadcast_to([128, 128]), OP.mult),
                reads=["nmst", "identb"], writes=["xqf"])
        for kc in range(8):
            pair = (PS[0], PS[1]) if kc % 2 == 0 else (PS[2], PS[3])
            kvar = ((ks_hi, "ks_hi", qs_hi, "qs_hi"),
                    (ks_lo, "ks_lo", qs_hi, "qs_hi"),
                    (ks_hi, "ks_hi", qs_lo, "qs_lo"))
            for i, (kb, kk, qb, qk) in enumerate(kvar):
                for hh in range(2):
                    p.op("tensor", lambda e, pair=pair, kb=kb, qb=qb, kc=kc, i=i, hh=hh: e.matmul(
                        pair[hh][:], kb[:, kc * 128:(kc + 1) * 128],
                        qb[:, hh * 512:(hh + 1) * 512], start=(i == 0), stop=False),
                        reads=[kk, qk], writes=[pkeys[id(pair[hh])]])
            for hh in range(2):
                p.op("tensor", lambda e, pair=pair, hh=hh: e.matmul(
                    pair[hh][:], ones_b[:], diagS[:, hh * 512:(hh + 1) * 512],
                    start=False, stop=True),
                    reads=["xqf", "ones_b"], writes=[pkeys[id(pair[hh])]])
                if s >= 7:
                    p.op("vector", lambda e, pair=pair, kc=kc, hh=hh: e.tensor_tensor(
                        pair[hh][:], pair[hh][:],
                        tri_T[:, (8 - kc) * 128 + hh * 512:(8 - kc) * 128 + (hh + 1) * 512], OP.add),
                        reads=[pkeys[id(pair[hh])], "tri_T"], writes=[pkeys[id(pair[hh])]])
                p.op("scalar", lambda e, pair=pair, hh=hh: e.activation(
                    pf[:, hh * 512:(hh + 1) * 512], pair[hh][:], AF.Exp, scale=SCALE),
                    reads=[pkeys[id(pair[hh])]], writes=["evA"])
            p.op("vector", lambda e, kc=kc: e.tensor_copy(pT_hi[:, kc * 1024:(kc + 1) * 1024], pf[:]),
                 reads=["evA"], writes=["pT_hi"], fence=bfa_fence)
            p.op("vector", lambda e, kc=kc: e.scalar_tensor_tensor(
                pT_lo[:, kc * 1024:(kc + 1) * 1024], pT_hi[:, kc * 1024:(kc + 1) * 1024], -1.0,
                pf[:], OP.mult, OP.add),
                reads=["pT_hi", "evA"], writes=["pT_lo"], fence=bfa_fence)
        for qt in range(8):
            pv = PS[4] if qt % 2 == 0 else PS[5]
            pvk = pkeys[id(pv)]
            nmm = 0
            for kc in range(8):
                for (pb, pk, vb, vk) in ((pT_hi, "pT_hi", vt_hi, "vt_hi"),
                                         (pT_hi, "pT_hi", vt_lo, "vt_lo"),
                                         (pT_lo, "pT_lo", vt_hi, "vt_hi")):
                    p.op("tensor", lambda e, pv=pv, pb=pb, vb=vb, kc=kc, qt=qt,
                         st=(nmm == 0), sp=(nmm == 23): e.matmul(
                        pv[:, 0:132], pb[:, kc * 1024 + qt * 128:kc * 1024 + (qt + 1) * 128],
                        vb[:, kc * 132:(kc + 1) * 132], start=st, stop=sp),
                        reads=[pk, vk], writes=[pvk])
                    nmm += 1
            p.op("vector", lambda e, pv=pv, qt=qt: e.tensor_tensor(
                O_loc[:, qt * 132:(qt + 1) * 132], pv[:, 0:132],
                O_loc[:, qt * 132:(qt + 1) * 132], OP.add),
                reads=[pvk, "FA"], writes=["FA"])
        for side in range(2):
            p.op("vector", lambda e, s=s, side=side: e.scalar_tensor_tensor(
                O_acc[side][:], O_loc[:], selw[:, side * 9 + s:side * 9 + s + 1],
                O_acc[side][:], OP.mult, OP.add),
                reads=["FA", "selw"], writes=["FA"])
        p.op("gpsimd", lambda e: e.memset(O_loc[:, 0:1056], 0.0), reads=["FA"], writes=["FA"])

    for g in range(16):
        Osl = O_acc[g // 8][:, (g % 8) * 132:(g % 8) * 132 + 132]
        p.op("vector", lambda e, Osl=Osl: e.reciprocal(colsc[:, 12:13], Osl[:, 128:129]),
             reads=["FA"], writes=["colsc12"])
        p.op("vector", lambda e, Osl=Osl: e.tensor_scalar(
            sm1[:, 0:128], Osl[:, 0:128], colsc[:, 12:13], None, OP.mult),
            reads=["FA", "colsc12"], writes=["sm1"])
        p.op("tensor", lambda e: e.transpose(psT[:, 0:128], sm1[:, 0:128], identf[:]),
             reads=["sm1", "identf"], writes=["PSb7"])
        p.op("scalar", lambda e: e.activation(bh1[:, 0:128], psT[:, 0:128], AF.Copy),
             reads=["PSb7"], writes=["bh1"])
        p.op("vector", lambda e: e.scalar_tensor_tensor(
            bh2[:, 0:128], bh1[:, 0:128], -1.0, psT[:, 0:128], OP.mult, OP.add),
            reads=["bh1", "PSb7"], writes=["bh2"])
        mmspec = ((bh1, "bh1", wot[0], "wot0"),
                  (bh1, "bh1", wot[1], "wot1"),
                  (bh2, "bh2", wot[0], "wot0"))
        for i, (ob, obk, wt, wtk) in enumerate(mmspec):
            for hh in range(2):
                p.op("tensor", lambda e, ob=ob, wt=wt, i=i, hh=hh: e.matmul(
                    PS[hh][:], ob[:, 0:128], wt[:, hh * 512:(hh + 1) * 512],
                    start=(i == 0), stop=False),
                    reads=[obk, wtk], writes=[pkeys[id(PS[hh])]])
        for hh in range(2):
            p.op("tensor", lambda e, hh=hh: e.matmul(
                PS[hh][:], ones_f[0:1, :], bo_row[:, hh * 512:(hh + 1) * 512],
                start=False, stop=True),
                reads=["bo_row", "ones_f"], writes=[pkeys[id(PS[hh])]])
        p.op("sync", lambda e, g=g: e.dma_start(out=evA[:], in_=part_h[g * 128:(g + 1) * 128, :]),
             reads=part_keys, writes=["evA"], dma=True)
        for hh in range(2):
            p.op("vector", lambda e, hh=hh: e.tensor_tensor(
                sm4[:, hh * 512:(hh + 1) * 512], PS[hh][:], evA[:, hh * 512:(hh + 1) * 512], OP.add),
                reads=[pkeys[id(PS[hh])], "evA"], writes=["sm4"])
        p.op("gpsimd", lambda e, g=g: e.dma_start(out=out_d[g * 128:(g + 1) * 128, :], in_=sm4[:]),
             reads=["sm4"], writes=[f"outH_{g}"], dma=True)

    with nc.Block() as block:
        p.emit(block)
    return nc


_NC_CACHE = None


def _get_nc():
    global _NC_CACHE
    if _NC_CACHE is None:
        _NC_CACHE = build_program()
    return _NC_CACHE


def _make_inmaps(x, wq, wk, wv, wo, bq, bk, bv, bo, w1, w2, cw, cb):
    in_maps = []
    for j in range(8):
        slots = [(0, bb) for bb in range(j)] + [(1, bb) for bb in range(7 - j)]
        slots += [(0, j), (1, 7 - j)]
        x_kv = np.concatenate([x[bat, bb * CH:(bb + 1) * CH] for bat, bb in slots], axis=0)
        x_kv = np.ascontiguousarray(x_kv)
        x_own = np.zeros((OWNP, D), np.float32)
        for cc, (bb, c0) in enumerate(((0, CH * j), (1, CH * (7 - j)))):
            lo, hi = c0 - 1, c0 + CH + 1
            slo, shi = max(lo, 0), min(hi, S)
            x_own[cc * GRID + (slo - lo): cc * GRID + (shi - lo)] = x[bb, slo:shi]
        sel = np.zeros((1, 64), np.int32)
        selw = np.zeros((128, 36), np.float32)
        selw[:, 18:36] = NEG
        for s, (bat, bb) in enumerate(slots):
            sel[0, 18 + s] = 0 if bat == 0 else CH
            selw[:, (0 if bat == 0 else 9) + s] = 1.0
            selw[:, 18 + (0 if bat == 0 else 9) + s] = 0.0
        in_maps.append(dict(
            x_kv=x_kv, x_own=x_own, wq=wq, wk=wk, wv=wv, wo=wo, bq=bq, bk=bk, bv=bv,
            bo=bo, w1=w1, w2=w2, cw=cw, cb=cb, seltab=sel, selw_in=selw))
    return in_maps


def kernel(L0_attn_x, L0_attn_Wq, L0_attn_bq, L0_attn_Wk, L0_attn_bk, L0_attn_Wv,
           L0_attn_bv, L0_attn_Wo, L0_attn_bo, L0_ffn_w1, L0_ffn_conv_w, L0_ffn_conv_b,
           L0_ffn_w2):
    x = np.ascontiguousarray(np.asarray(L0_attn_x, dtype=np.float32))
    wq = np.ascontiguousarray(np.asarray(L0_attn_Wq, dtype=np.float32))
    wk = np.ascontiguousarray(np.asarray(L0_attn_Wk, dtype=np.float32))
    wv = np.ascontiguousarray(np.asarray(L0_attn_Wv, dtype=np.float32))
    wo = np.ascontiguousarray(np.asarray(L0_attn_Wo, dtype=np.float32))
    bq = np.asarray(L0_attn_bq, dtype=np.float32).reshape(1, H)
    bk = np.asarray(L0_attn_bk, dtype=np.float32).reshape(1, H)
    bv = np.asarray(L0_attn_bv, dtype=np.float32).reshape(1, H)
    bo = np.asarray(L0_attn_bo, dtype=np.float32).reshape(1, D)
    w1 = np.ascontiguousarray(np.asarray(L0_ffn_w1, dtype=np.float32))
    w2 = np.ascontiguousarray(np.asarray(L0_ffn_w2, dtype=np.float32))
    cw3 = np.asarray(L0_ffn_conv_w, dtype=np.float32)[:, 0, :]
    cw = np.ascontiguousarray(
        cw3.reshape(32, 128, 3).transpose(1, 2, 0).reshape(128, 96))
    cb = np.ascontiguousarray(
        np.asarray(L0_ffn_conv_b, dtype=np.float32).reshape(32, 128).T)

    in_maps = _make_inmaps(x, wq, wk, wv, wo, bq, bk, bv, bo, w1, w2, cw, cb)
    nc = _get_nc()
    res = run_bass_kernel_spmd(nc, in_maps, core_ids=list(range(8)))
    out = np.empty((B, S, D), np.float32)
    for j in range(8):
        o = np.asarray(res.results[j]["out_own"])
        out[0, CH * j:CH * (j + 1)] = o[0:CH]
        out[1, CH * (7 - j):CH * (8 - j)] = o[CH:2 * CH]
    return out
